# version 14
# baseline (speedup 1.0000x reference)
import numpy as np
from contextlib import ExitStack, contextmanager

import concourse.bass as bass
import concourse.mybir as mybir
from concourse.bass_utils import run_bass_kernel_spmd

F32 = mybir.dt.float32
BF16 = mybir.dt.bfloat16
AF = mybir.ActivationFunctionType
ALU = mybir.AluOpType
AX = mybir.AxisListType

H = 16
D = 128
S = 2048
DM = 2048
DFF = 5504
NDC = DM // 128
NFC = DFF // 128
NB = 32
SM = 128 ** -0.5
BIG = 1.0e30
EPS = 1e-5

ENGINES = ("tensor", "vector", "scalar", "gpsimd", "sync")
COMPUTE_SELFSYNC = ("vector", "scalar", "gpsimd")


class Tracker:
    def __init__(self, nc, dma_depth=4):
        self.nc = nc
        self.ops = {e: [] for e in ENGINES}
        self.cnt = {e: 0 for e in ENGINES}
        self.dma_depth = dma_depth
        self.dq = {q: {"batch": 0, "targets": [0] * dma_depth}
                   for q in ("sync", "scalar", "gpsimd")}
        self.last_write = {}
        self.readers = {}
        self.watermark = {e: {} for e in ENGINES}
        self._batch_ctx = None
        self.n_waits = 0

    def _label_val(self, label):
        if label.startswith("E:"):
            return self.cnt[label[2:]]
        _, q, s = label.split(":")
        return self.dq[q]["targets"][int(s)]

    def _edges(self, eng, reads, writes, self_label):
        waits = {}

        def need(label, val):
            if label == self_label and not (eng in COMPUTE_SELFSYNC and label.startswith("E:")):
                return
            if label == "E:tensor" and eng == "tensor":
                return
            if self.watermark[eng].get(label, 0) >= val:
                return
            if waits.get(label, 0) < val:
                waits[label] = val

        for k in reads:
            w = self.last_write.get(k)
            if w:
                need(*w)
        for k in writes:
            w = self.last_write.get(k)
            if w:
                need(*w)
            for label, val in self.readers.get(k, {}).items():
                need(label, val)
        for label, val in waits.items():
            self.watermark[eng][label] = val
        self.n_waits += len(waits)
        return waits

    def _record(self, label, reads, writes):
        v = self._label_val(label)
        for k in writes:
            self.last_write[k] = (label, v)
            self.readers[k] = {}
        for k in reads:
            self.readers.setdefault(k, {})[label] = v

    def emit(self, eng, fn, reads=(), writes=(), tag=None):
        self_label = f"E:{eng}"
        waits = self._edges(eng, reads, writes, self_label)
        self.cnt[eng] += 1
        self.ops[eng].append((waits, fn, self_label, 1, tag))
        self._record(self_label, reads, writes)
        return self.cnt[eng]

    @contextmanager
    def dma_batch(self, queue):
        assert self._batch_ctx is None
        q = self.dq[queue]
        s = q["batch"] % self.dma_depth
        q["batch"] += 1
        gate = q["targets"][s]
        self._batch_ctx = (queue, s, [], [], gate)
        try:
            yield
        finally:
            queue_, s_, rd, wr, _ = self._batch_ctx
            self._batch_ctx = None
            self._record(f"D:{queue_}:{s_}", rd, wr)

    def dma(self, queue, out, in_, reads=(), writes=(), tag=None, **kw):
        standalone = self._batch_ctx is None
        if standalone:
            ctx = self.dma_batch(queue)
            ctx.__enter__()
        queue_, s, rd, wr, gate = self._batch_ctx
        assert queue_ == queue
        label = f"D:{queue}:{s}"
        waits = self._edges(queue, reads, writes, label)
        if gate > 0 and self.watermark[queue].get(label, 0) < gate:
            waits[label] = max(waits.get(label, 0), gate)
            self.watermark[queue][label] = gate
        self.dq[queue]["targets"][s] += 16
        self.ops[queue].append(
            (waits, lambda h, o=out, i=in_, kw=kw: h.dma_start(out=o, in_=i, **kw),
             label, 16, tag))
        rd.extend(reads)
        wr.extend(writes)
        if standalone:
            ctx.__exit__(None, None, None)

    def barrier(self, eng, labels=None):
        if labels is None:
            labels = {}
            for e in ENGINES:
                if e != eng and self.cnt[e]:
                    labels[f"E:{e}"] = self.cnt[e]
            for qname, q in self.dq.items():
                for s, t in enumerate(q["targets"]):
                    if t:
                        labels[f"D:{qname}:{s}"] = t
        waits = {}
        for label, val in labels.items():
            if self.watermark[eng].get(label, 0) < val:
                waits[label] = val
                self.watermark[eng][label] = val
        if waits:
            self.ops[eng].append((waits, None, None, 0, "barrier"))

    def phase(self):
        for e in ENGINES:
            self.barrier(e)

    def build(self):
        nc = self.nc
        labels = set()
        for e in ENGINES:
            for waits, _, inc_label, _, _ in self.ops[e]:
                labels.update(waits)
                if inc_label:
                    labels.add(inc_label)
        with ExitStack() as stack:
            sems = {lb: stack.enter_context(nc.semaphore(lb.replace(":", "_")))
                    for lb in sorted(labels)}
            block = stack.enter_context(nc.Block())
            for eng in ENGINES:
                if not self.ops[eng]:
                    continue

                def make(e, ops_e):
                    def run(handle):
                        for waits, fn, inc_label, inc_amt, tag in ops_e:
                            for lb, v in sorted(waits.items()):
                                handle.wait_ge(sems[lb], v)
                            if fn is not None:
                                ins = fn(handle)
                                if inc_label is not None:
                                    ins.then_inc(sems[inc_label], inc_amt)
                    return run

                getattr(block, eng)(make(eng, self.ops[eng]))
        return sems


def host_consts():
    ext = np.zeros((128, 64), np.float32)
    for p in range(128):
        t_self = 32 + (p // 64)
        ext[p, t_self] = BIG
        ext[p, t_self + 1:] = -BIG
    causal = np.zeros((128, 128), np.float32)
    for q in range(128):
        causal[q, q + 1:] = -BIG
    ident = np.eye(128, dtype=np.float32)
    return np.concatenate([ext, causal, ident], axis=1)


def build_program(dbg=()):
    nc = bass.Bass("TRN2", target_bir_lowering=False)
    x_h = nc.dram_tensor("x", [S, DM], F32, kind="ExternalInput").ap()
    wq_h = nc.dram_tensor("wq", [DM, DM], F32, kind="ExternalInput").ap()
    wk_h = nc.dram_tensor("wk", [DM, DM], F32, kind="ExternalInput").ap()
    wv_h = nc.dram_tensor("wv", [DM, DM], F32, kind="ExternalInput").ap()
    wo_h = nc.dram_tensor("wo", [DM, DM], F32, kind="ExternalInput").ap()
    wg_h = nc.dram_tensor("wg", [DFF, DM], F32, kind="ExternalInput").ap()
    wu_h = nc.dram_tensor("wu", [DFF, DM], F32, kind="ExternalInput").ap()
    wd_h = nc.dram_tensor("wd", [DM, DFF], F32, kind="ExternalInput").ap()
    co_h = nc.dram_tensor("consts", [128, 320], F32, kind="ExternalInput").ap()
    out_h = nc.dram_tensor("out", [S, DM], F32, kind="ExternalOutput").ap()

    qs = nc.dram_tensor("qs", [H, 128, S], F32, kind="Internal").ap()
    ks = nc.dram_tensor("ks", [H, 128, S], F32, kind="Internal").ap()
    vs = nc.dram_tensor("vs", [H, 128, S], F32, kind="Internal").ap()
    os_ = nc.dram_tensor("os", [4, H, 128, 512], F32, kind="Internal").ap()
    x1s = nc.dram_tensor("x1s", [4, NDC, 128, 512], F32, kind="Internal").ap()
    xqs = nc.dram_tensor("xqs", [4, NDC, 128, 512], BF16, kind="Internal").ap()
    hs = nc.dram_tensor("hs", [NFC, 128, S], F32, kind="Internal").ap()
    wds = nc.dram_tensor("wds", [NDC, 2, NFC, 128, 128], BF16, kind="Internal").ap()
    ss1 = nc.dram_tensor("ss1", [4, 128, 512], F32, kind="Internal").ap()

    dbg_shapes = {"qs": [H, 128, S], "ks": [H, 128, S], "vs": [H, 128, S],
                  "os": [4, H, 128, 512], "x1s": [4, NDC, 128, 512],
                  "xqs": [4, NDC, 128, 512], "hs": [NFC, 128, S],
                  "probe": [4, 128, 2048]}
    dbg_src = {"qs": qs, "ks": ks, "vs": vs, "os": os_, "x1s": x1s,
               "xqs": xqs, "hs": hs, "wds": wds}
    dbg_aps = {n: nc.dram_tensor("dbg_" + n, dbg_shapes[n], F32,
                                 kind="ExternalOutput").ap() for n in dbg}
    probe_ap = dbg_aps.get("probe")

    def dbg_dump(T, names):
        for n in names:
            if n in dbg_aps:
                T.barrier("gpsimd")
                T.dma("gpsimd", dbg_aps[n][:], dbg_src[n][:], tag="dbg_" + n)

    AW = 49152
    with ExitStack() as st:
        e = st.enter_context
        AR1 = e(nc.sbuf_tensor("arena1", [128, 24576], F32))
        WDTBH = e(nc.sbuf_tensor("wdtbh", [128, 2048], BF16))
        WDTBL = e(nc.sbuf_tensor("wdtbl", [128, 2048], BF16))
        CO = e(nc.sbuf_tensor("co", [128, 320], F32))
        KG = e(nc.sbuf_tensor("kg", [128, H * NB], F32))
        GW = e(nc.sbuf_tensor("gw", [128, 512], F32))
        T8 = e(nc.sbuf_tensor("t8", [128, 128], F32))
        BB = e(nc.sbuf_tensor("bb", [128, 512], F32))
        SMALL = e(nc.sbuf_tensor("small", [128, 160], F32))
        SM2 = e(nc.sbuf_tensor("small2", [128, 256], F32))
        ONES = e(nc.sbuf_tensor("ones", [128, 128], F32))
        MREP = e(nc.sbuf_tensor("mrep", [128, 512], F32))
        RREP = e(nc.sbuf_tensor("rrep", [128, 512], F32))
        PSA = e(nc.psum_tensor("psA", [128, 2048], F32))
        PSB = e(nc.psum_tensor("psB", [128, 2048], F32))

        ident = CO[:, 192:320]
        causal = CO[:, 64:192]

        def psb(i):
            return PSB[:, 512 * i:512 * (i + 1)]

        T = Tracker(nc)
        T.dma("sync", CO[:], co_h[:, :], writes=["co"])
        T.emit("vector", lambda v: v.memset(ONES[:], 1.0), writes=["ones"])
        T.emit("vector", lambda v: v.memset(SM2[:], 0.0),
               writes=["amax1", "amax2", "q_s0", "q_sc", "q_thr", "q_nthr",
                       "amaxT", "amax2T"])
        T.emit("vector", lambda v: v.memset(SMALL[:], 0.0),
               writes=["epsc", "s_sum", "s_sq", "s_m", "s_m2", "s_v", "s_sq2", "s_r0",
                       "s_rr", "s_hv", "s_t", "s_r1", "a_m0", "a_m1", "a_m2",
                       "a_sig0", "a_sig1", "a_sig2", "a_inv0", "a_inv1", "a_inv2"])
        EPSC = SMALL[:, 159:160]
        T.emit("vector", lambda v: v.memset(EPSC, EPS), writes=["epsc"])

        def ladder(buf, tmp, key_buf, width, tmp_key, out=None, out_key=None):
            z0 = tmp[:, 0:width]
            tt = tmp[:, width:2 * width]
            kb = [key_buf] if isinstance(key_buf, str) else list(key_buf)
            tk = tmp_key
            T.emit("vector", lambda v: v.tensor_scalar(buf, buf, 128.5, None, op0=ALU.add),
                   reads=kb, writes=kb)
            T.emit("vector", lambda v: v.tensor_copy(z0, buf),
                   reads=kb, writes=[tk])
            for c in (256.0, 128.0, 64.0, 32.0, 16.0, 8.0, 4.0, 2.0, 1.0):
                T.emit("vector", lambda v, c=c: v.tensor_scalar(tt, buf, c, c,
                                                                op0=ALU.is_ge, op1=ALU.mult),
                       reads=kb, writes=[tk])
                T.emit("vector", lambda v: v.tensor_sub(buf, buf, tt),
                       reads=kb + [tk], writes=kb)
            T.emit("vector", lambda v: v.tensor_sub(buf, z0, buf),
                   reads=kb + [tk], writes=kb)
            dst = buf if out is None else out
            dk = kb if out_key is None else [out_key]
            T.emit("vector", lambda v: v.tensor_scalar(dst, buf, 128.0, None,
                                                       op0=ALU.subtract),
                   reads=kb, writes=dk)

        def ternary_quant(wbuf, key_w, tq_buf, key_tq, ngrp, sink=None):
            if sink is None:
                emit = T.emit
            else:
                emit = lambda *a, **kw: sink.append(lambda a=a, kw=kw: T.emit(*a, **kw))
            w3 = wbuf.rearrange("p (g w) -> p g w", w=128)
            emit("vector", lambda v: v.reduce_sum(SM2[:, 0:ngrp], w3, axis=AX.X,
                                                  apply_absolute_value=True),
                 reads=[key_w], writes=["q_s0"])
            emit("vector", lambda v: v.tensor_scalar(SM2[:, 64:64 + ngrp], SM2[:, 0:ngrp],
                                                     1.0 / 128, EPS,
                                                     op0=ALU.mult, op1=ALU.add),
                 reads=["q_s0"], writes=["q_sc"])
            emit("vector", lambda v: v.tensor_scalar(SM2[:, 128:128 + ngrp],
                                                     SM2[:, 64:64 + ngrp], 0.5, None,
                                                     op0=ALU.mult),
                 reads=["q_sc"], writes=["q_thr"])
            emit("vector", lambda v: v.tensor_scalar(SM2[:, 192:192 + ngrp],
                                                     SM2[:, 128:128 + ngrp], -1.0, None,
                                                     op0=ALU.mult),
                 reads=["q_thr"], writes=["q_nthr"])
            t3 = tq_buf.rearrange("p (g w) -> p g w", w=128)
            emit("vector", lambda v: v.tensor_tensor(
                t3, w3, SM2[:, 128:128 + ngrp].rearrange("p (g o) -> p g o", o=1)
                .broadcast_to([128, ngrp, 128]), op=ALU.is_ge),
                reads=[key_w, "q_thr"], writes=[key_tq])
            emit("vector", lambda v: v.tensor_tensor(
                w3, w3, SM2[:, 192:192 + ngrp].rearrange("p (g o) -> p g o", o=1)
                .broadcast_to([128, ngrp, 128]), op=ALU.is_le),
                reads=[key_w, "q_nthr"], writes=[key_w])
            emit("vector", lambda v: v.tensor_sub(tq_buf, tq_buf, wbuf),
                 reads=[key_tq, key_w], writes=[key_tq])
            emit("vector", lambda v: v.tensor_tensor(
                t3, t3, SM2[:, 64:64 + ngrp].rearrange("p (g o) -> p g o", o=1)
                .broadcast_to([128, ngrp, 128]), op=ALU.mult),
                reads=[key_tq, "q_sc"], writes=[key_tq])

        def hilo_split(tq_buf, key_tq, hi_buf, key_hi, lo_buf, key_lo, ngrp):
            n = ngrp * 128
            T.emit("vector", lambda v: v.tensor_copy(hi_buf[:, 0:n], tq_buf[:, 0:n]),
                   reads=[key_tq], writes=[key_hi])
            T.emit("vector", lambda v: v.tensor_sub(lo_buf[:, 0:n], tq_buf[:, 0:n],
                                                    hi_buf[:, 0:n]),
                   reads=[key_tq, key_hi], writes=[key_lo])

        AR2_ctx = nc.sbuf_tensor("arena2", [128, 22528], F32)
        AR2 = AR2_ctx.__enter__()
        ych = [(AR1[:, 2048 * c:2048 * (c + 1)] if c < 12
                else AR2[:, 2048 * (c - 12):2048 * (c - 11)]) for c in range(NDC)]
        yq4 = [(AR1[:, 8192 * q:8192 * (q + 1)] if q < 3 else AR2[:, 0:8192])
               .rearrange("p (c n) -> p c n", c=4) for q in range(4)]
        xt = AR2[:, 8192:16384].rearrange("p (c n) -> p c n", c=4)
        stg = [AR2[:, 16384 + 2048 * i:16384 + 2048 * (i + 1)] for i in range(2)]

        for ti in range(4):
            tok0 = 512 * ti
            T.dma("sync", xt[:, :, :],
                  x_h[tok0:tok0 + 512, :].rearrange("(c p) n -> p c n", p=128),
                  writes=["xt"], tag=f"ldx{ti}")
            for cc in range(4):
                xc = xt[:, cc, :]
                sb = cc % 2
                T.emit("vector", lambda v, xc=xc: v.reduce_sum(
                    SMALL[:, 0:1], xc, axis=AX.X), reads=["xt"], writes=["s_sum"])
                T.emit("scalar", lambda s, xc=xc, sb=sb: s.activation(
                    stg[sb][:], xc, AF.Square, accum_out=SMALL[:, 1:2]),
                    reads=["xt"], writes=[f"stg{sb}", "s_sq"])
                T.emit("vector", lambda v: v.tensor_scalar(
                    SMALL[:, 2:3], SMALL[:, 0:1], 1.0 / DM, None, op0=ALU.mult),
                    reads=["s_sum"], writes=["s_m"])
                T.emit("vector", lambda v: v.tensor_scalar(
                    SMALL[:, 3:4], SMALL[:, 2:3], SMALL[:, 2:3], None, op0=ALU.mult),
                    reads=["s_m"], writes=["s_m2"])
                T.emit("vector", lambda v: v.tensor_scalar(
                    SMALL[:, 4:5], SMALL[:, 1:2], 1.0 / DM, SMALL[:, 3:4],
                    op0=ALU.mult, op1=ALU.subtract),
                    reads=["s_sq", "s_m2"], writes=["s_v"])
                T.emit("scalar", lambda s: s.activation(
                    SMALL[:, 5:6], SMALL[:, 4:5], AF.Sqrt, bias=EPSC),
                    reads=["s_v", "epsc"], writes=["s_sq2"])
                T.emit("vector", lambda v: v.reciprocal(SMALL[:, 6:7], SMALL[:, 5:6]),
                       reads=["s_sq2"], writes=["s_r0"])
                T.emit("vector", lambda v: v.tensor_scalar(
                    SMALL[:, 7:8], SMALL[:, 6:7], SMALL[:, 6:7], None, op0=ALU.mult),
                    reads=["s_r0"], writes=["s_rr"])
                T.emit("vector", lambda v: v.tensor_scalar(
                    SMALL[:, 8:9], SMALL[:, 4:5], 0.5, 0.5 * EPS,
                    op0=ALU.mult, op1=ALU.add), reads=["s_v"], writes=["s_hv"])
                T.emit("vector", lambda v: v.tensor_scalar(
                    SMALL[:, 9:10], SMALL[:, 7:8], SMALL[:, 8:9], 1.5,
                    op0=ALU.mult, op1=ALU.subtract), reads=["s_rr", "s_hv"],
                    writes=["s_t"])
                T.emit("vector", lambda v: v.tensor_scalar(
                    SMALL[:, 9:10], SMALL[:, 9:10], -1.0, None, op0=ALU.mult),
                    reads=["s_t"], writes=["s_t"])
                T.emit("vector", lambda v: v.tensor_scalar(
                    SMALL[:, 10:11], SMALL[:, 6:7], SMALL[:, 9:10], None, op0=ALU.mult),
                    reads=["s_r0", "s_t"], writes=["s_r1"])
                T.emit("vector", lambda v, xc=xc: v.tensor_scalar(
                    xc, xc, SMALL[:, 2:3], SMALL[:, 10:11],
                    op0=ALU.subtract, op1=ALU.mult),
                    reads=["xt", "s_m", "s_r1"], writes=["xt"])
                tok = tok0 + cc * 128
                for fq in range(4):
                    for ff in range(4):
                        fc = fq * 4 + ff
                        T.emit("tensor", lambda t, xc=xc, fc=fc, ff=ff: t.transpose(
                            psb(0)[:, 128 * ff:128 * (ff + 1)],
                            xc[:, 128 * fc:128 * (fc + 1)], ident),
                            reads=["xt", "co"], writes=["psB0"])
                    T.emit("vector", lambda v, fq=fq, tok=tok: v.tensor_copy(
                        yq4[fq][:, 0:4, tok:tok + 128],
                        psb(0)[:, 0:512].rearrange("p (c n) -> p c n", c=4)),
                        reads=["psB0"], writes=["y"])

        wbuf = [AR2[:, 8192 + 4096 * i:8192 + 4096 * (i + 1)]
                .rearrange("p (c n) -> p c n", c=NDC) for i in range(2)]
        for wi, (w_hh, scr, nm) in enumerate(((wq_h, qs, "q"), (wk_h, ks, "k"),
                                              (wv_h, vs, "v"))):
            for dh in range(8):
                bi = (wi * 8 + dh) % 2
                al = ["xt"] if (wi == 0 and dh < 2) else []
                T.dma("sync", wbuf[bi][:, :, :],
                      w_hh[:, 256 * dh:256 * (dh + 1)].rearrange("(c p) n -> p c n", p=128),
                      writes=[f"wb{bi}"] + al, tag=f"ldw{nm}{dh}")
                for sub in range(2):
                    dc = dh * 2 + sub
                    for ns in range(4):
                        for din in range(NDC):
                            T.emit("tensor", lambda t, bi=bi, sub=sub, ns=ns, din=din: t.matmul(
                                PSA[:, 512 * ns:512 * (ns + 1)],
                                wbuf[bi][:, din, 128 * sub:128 * (sub + 1)],
                                ych[din][:, 512 * ns:512 * (ns + 1)],
                                start=(din == 0), stop=(din == NDC - 1)),
                                reads=[f"wb{bi}", "y"], writes=["psA"], tag=f"mm{nm}{dc}")
                    sb = dc % 2
                    if nm == "q":
                        T.emit("vector", lambda v, sb=sb: v.tensor_scalar(
                            stg[sb][:], PSA[:], SM, None, op0=ALU.mult),
                            reads=["psA"], writes=[f"stg{sb}"])
                    else:
                        T.emit("vector", lambda v, sb=sb: v.tensor_copy(stg[sb][:], PSA[:]),
                               reads=["psA"], writes=[f"stg{sb}"])
                    if nm == "k":
                        T.emit("vector", lambda v, sb=sb, dc=dc: v.reduce_sum(
                            KG[:, NB * dc:NB * (dc + 1)],
                            stg[sb][:].rearrange("p (b w) -> p b w", w=64), axis=AX.X),
                            reads=[f"stg{sb}"], writes=["kg"])
                    T.dma("sync", scr[dc, :, :], stg[sb][:], reads=[f"stg{sb}"],
                          writes=[f"{nm}sc{dc}"], tag=f"st{nm}{dc}")

        dbg_dump(T, ["qs", "ks", "vs"])
        T.phase()

        qh = AR1[:, 0:2048]
        kh = [AR1[:, 2048:4096], AR1[:, 4096:6144]]
        vh = [AR1[:, 6144:8192], AR1[:, 8192:10240]]
        vt = AR1[:, 10240:12288]
        Pt = AR1[:, 12288:14336]
        PT = AR1[:, 14336:22528].rearrange("p (c n) -> p c n", c=16)
        oh = AR1[:, 22528:24576]
        wdl = [AR2[:, 5504 * i:5504 * (i + 1)] for i in range(2)]
        Pt2 = AR2[:, 11008:13056]
        wdq = AR2[:, 13056:18560]
        Pt3 = AR2[:, 18560:20608]
        Pts = [Pt, Pt2, Pt3]

        pending = []
        for h in range(H):
            b = h % 2
            with T.dma_batch("sync"):
                T.dma("sync", qh[:], qs[h, :, :], reads=[f"qsc{h}"], writes=["qh"],
                      tag=f"ldq{h}")
                T.dma("sync", kh[b][:], ks[h, :, :], reads=[f"ksc{h}"], writes=[f"kh{b}"],
                      tag=f"ldk{h}")
                T.dma("sync", vh[b][:], vs[h, :, :], reads=[f"vsc{h}"], writes=[f"vh{b}"],
                      tag=f"ldv{h}")
            al = ["y"] if h < 2 else []
            T.dma("sync", wdl[b][:], wd_h[128 * h:128 * (h + 1), :],
                  writes=[f"wdl{b}"] + al, tag=f"ldwd{h}")
            for j in range(16):
                T.emit("tensor", lambda t, j=j, h=h: t.matmul(
                    psb(0)[:, 32 * j:32 * (j + 1)], qh[:, 128 * j:128 * (j + 1)],
                    KG[:, NB * h:NB * (h + 1)], start=(j == 0), stop=(j == 15)),
                    reads=["qh", "kg"], writes=["psB0"], tag=f"gate{h}")
            for j in range(16):
                T.emit("vector", lambda v, j=j: v.scalar_tensor_tensor(
                    GW[:, 32 * j:32 * (j + 1)], psb(0)[:, 32 * j:32 * (j + 1)], 1.0 / 64,
                    CO[:, 32 - 2 * j:64 - 2 * j], op0=ALU.mult, op1=ALU.add),
                    reads=["psB0", "co"], writes=["gw"])
                T.emit("vector", lambda v, j=j: v.max(
                    T8[:, 8 * j:8 * (j + 1)], GW[:, 32 * j:32 * (j + 1)]),
                    reads=["gw"], writes=["t8"])
                T.emit("vector", lambda v, j=j: v.tensor_scalar(
                    BB[:, 32 * j:32 * (j + 1)], GW[:, 32 * j:32 * (j + 1)],
                    T8[:, 8 * j + 7:8 * j + 8], -BIG, op0=ALU.is_lt, op1=ALU.mult),
                    reads=["gw", "t8"], writes=["bb"])
            for kc4 in range(4):
                for ff in range(4):
                    kc = kc4 * 4 + ff
                    T.emit("tensor", lambda t, b=b, kc=kc, ff=ff: t.transpose(
                        psb(1)[:, 128 * ff:128 * (ff + 1)],
                        vh[b][:, 128 * kc:128 * (kc + 1)], ident),
                        reads=[f"vh{b}", "co"], writes=["psB1"])
                T.emit("scalar", lambda s, kc4=kc4: s.activation(
                    vt[:, 512 * kc4:512 * (kc4 + 1)], psb(1)[:, 0:512], AF.Copy),
                    reads=["psB1"], writes=["vt"])
            slice_ctr = [0]

            def emit_A(j):
                kend = 128 * (j + 1)
                par = j % 3
                ptb = Pts[par]
                nsl = list(range(0, kend, 512))
                for n0 in nsl:
                    n1 = min(kend, n0 + 512)
                    bk = slice_ctr[0] % 4
                    slice_ctr[0] += 1
                    last = (n0 == nsl[-1])
                    psl = PSA[:, 512 * bk:512 * bk + (n1 - n0)]
                    T.emit("tensor", lambda t, b=b, j=j, n0=n0, n1=n1, psl=psl,
                           last=last: t.matmul(
                        psl, qh[:, 128 * j:128 * (j + 1)], kh[b][:, n0:n1],
                        start=True, stop=not last),
                        reads=["qh", f"kh{b}"], writes=[f"psA{bk}"], tag=f"sc{h}_{j}")
                    if last:
                        w = n1 - n0
                        T.emit("tensor", lambda t, psl=psl, w=w: t.matmul(
                            psl[:, w - 128:w], ident, causal,
                            start=False, stop=True),
                            reads=["co"], writes=[f"psA{bk}"], tag=f"cz{h}_{j}")
                    nb0 = n0 // 64
                    nbn = (n1 - n0) // 64
                    T.emit("vector", lambda v, j=j, n0=n0, n1=n1, psl=psl, par=par,
                           ptb=ptb, nb0=nb0, nbn=nbn: v.tensor_tensor(
                        ptb[:, n0:n1].rearrange("p (b w) -> p b w", w=64),
                        psl.rearrange("p (b w) -> p b w", w=64),
                        BB[:, 32 * j + nb0:32 * j + nb0 + nbn]
                        .rearrange("p (b o) -> p b o", o=1)
                        .broadcast_to([128, nbn, 64]),
                        op=ALU.add), reads=[f"psA{bk}", "bb"], writes=[f"Pt{par}"])
                T.emit("vector", lambda v, kend=kend, par=par, ptb=ptb: v.reduce_max(
                    SMALL[:, 20 + 3 * par:21 + 3 * par], ptb[:, 0:kend], axis=AX.X,
                    negate=True), reads=[f"Pt{par}"], writes=[f"a_m{par}"])
                T.emit("scalar", lambda s, kend=kend, par=par, ptb=ptb: s.activation(
                    ptb[:, 0:kend], ptb[:, 0:kend], AF.Exp,
                    bias=SMALL[:, 20 + 3 * par:21 + 3 * par], scale=1.0,
                    accum_out=SMALL[:, 21 + 3 * par:22 + 3 * par]),
                    reads=[f"Pt{par}", f"a_m{par}"],
                    writes=[f"Pt{par}", f"a_sig{par}"])

            def emit_B(jb):
                kend = 128 * (jb + 1)
                par = jb % 3
                jjb = jb % 4
                ptb = Pts[par]
                T.emit("vector", lambda v, par=par: v.reciprocal(
                    SMALL[:, 22 + 3 * par:23 + 3 * par],
                    SMALL[:, 21 + 3 * par:22 + 3 * par]),
                    reads=[f"a_sig{par}"], writes=[f"a_inv{par}"])
                T.emit("vector", lambda v, kend=kend, par=par, ptb=ptb: v.tensor_scalar(
                    ptb[:, 0:kend], ptb[:, 0:kend],
                    SMALL[:, 22 + 3 * par:23 + 3 * par], None, op0=ALU.mult),
                    reads=[f"Pt{par}", f"a_inv{par}"], writes=[f"Pt{par}"])
                for kc0 in range(0, jb + 1, 4):
                    kcn = min(4, jb + 1 - kc0)
                    bk = 2 + ((kc0 // 4) % 2)
                    for ff in range(kcn):
                        kc = kc0 + ff
                        T.emit("tensor", lambda t, kc=kc, ff=ff, ptb=ptb, bk=bk: t.transpose(
                            psb(bk)[:, 128 * ff:128 * (ff + 1)],
                            ptb[:, 128 * kc:128 * (kc + 1)], ident),
                            reads=[f"Pt{par}", "co"], writes=[f"psB{bk}"])
                    T.emit("scalar", lambda s, kc0=kc0, kcn=kcn, jjb=jjb, bk=bk: s.activation(
                        PT[:, kc0:kc0 + kcn, 128 * jjb:128 * (jjb + 1)],
                        psb(bk)[:, 0:128 * kcn].rearrange("p (c n) -> p c n", c=kcn),
                        AF.Copy),
                        reads=[f"psB{bk}"], writes=["PT"])

            def emit_PV(g):
                nkc_g = 4 * g + 4
                for kc in range(nkc_g):
                    col0 = 128 * max(0, kc - 4 * g)
                    T.emit("tensor", lambda t, kc=kc, col0=col0, nkc_g=nkc_g: t.matmul(
                        psb(3)[:, col0:512], vt[:, 128 * kc:128 * (kc + 1)],
                        PT[:, kc, col0:512],
                        start=(kc == 0), stop=(kc == nkc_g - 1)),
                        reads=["vt", "PT"], writes=["psB3"], tag=f"pv{h}_{g}")
                T.emit("scalar", lambda s, g=g: s.activation(
                    oh[:, 512 * g:512 * (g + 1)], psb(3)[:, 0:512], AF.Copy),
                    reads=["psB3"], writes=["oh"])
                T.dma("sync", os_[g, h, :, :], oh[:, 512 * g:512 * (g + 1)],
                      reads=["oh"], writes=[f"ossc{g}_{h}"], tag=f"sto{h}_{g}")

            for j in range(16):
                jb = j - 2
                if jb >= 0:
                    emit_B(jb)
                    if jb % 4 == 3:
                        emit_PV(jb // 4)
                if pending:
                    pending.pop(0)()
                emit_A(j)
                if pending:
                    pending.pop(0)()
            emit_B(14)
            emit_B(15)
            emit_PV(3)
            while pending:
                pending.pop(0)()
            dc = h
            ternary_quant(wdl[b], f"wdl{b}", wdq, "wdq", NFC, sink=pending)

            def wd_evac(fc0, dc=dc):
                fcn = min(4, NFC - fc0)
                for ff in range(fcn):
                    fc = fc0 + ff
                    T.emit("tensor", lambda t, fc=fc, ff=ff: t.transpose(
                        psb(1)[:, 128 * ff:128 * (ff + 1)],
                        wdq[:, 128 * fc:128 * (fc + 1)], ident),
                        reads=["wdq", "co"], writes=["psB1"])
                T.emit("scalar", lambda s, fcn=fcn: s.activation(
                    WDTBH[:, 0:128 * fcn], psb(1)[:, 0:128 * fcn], AF.Copy),
                    reads=["psB1"], writes=["wdtbh"])
                T.emit("vector", lambda v, fcn=fcn: v.tensor_sub(
                    WDTBL[:, 0:128 * fcn], psb(1)[:, 0:128 * fcn], WDTBH[:, 0:128 * fcn]),
                    reads=["psB1", "wdtbh"], writes=["wdtbl"])
                T.dma("sync", wds[dc, 0, fc0:fc0 + fcn, :, :].rearrange("c p n -> p c n"),
                      WDTBH[:, 0:128 * fcn].rearrange("p (c n) -> p c n", c=fcn),
                      reads=["wdtbh"], writes=[f"wdsc{dc}"], tag=f"stwdh{dc}_{fc0}")
                T.dma("sync", wds[dc, 1, fc0:fc0 + fcn, :, :].rearrange("c p n -> p c n"),
                      WDTBL[:, 0:128 * fcn].rearrange("p (c n) -> p c n", c=fcn),
                      reads=["wdtbl"], writes=[f"wdsc{dc}"], tag=f"stwdl{dc}_{fc0}")

            for fc0 in range(0, NFC, 4):
                pending.append(lambda fc0=fc0: wd_evac(fc0))
        while pending:
            pending.pop(0)()

        dbg_dump(T, ["os", "wds"])
        T.phase()

        osg = AR1[:, 0:8192].rearrange("p (c n) -> p c n", c=16)
        xtm = AR1[:, 8192:16384].rearrange("p (c n) -> p c n", c=4)
        xtmf = AR1[:, 8192:16384]
        xfm2 = [AR1[:, 16384:24576].rearrange("p (c n) -> p c n", c=16),
                AR2[:, 12288:20480].rearrange("p (c n) -> p c n", c=16)]
        xfm2f = [AR1[:, 16384:24576], AR2[:, 12288:20480]]
        x1g = AR2[:, 0:8192].rearrange("p (c n) -> p c n", c=16)
        woc = [AR2[:, 8192 + 2048 * i:8192 + 2048 * (i + 1)]
               .rearrange("p (c n) -> p c n", c=16) for i in range(2)]
        SR3 = AR2[:, 20480:22528]

        def p3_xload(g):
            par = g % 2
            xf = xfm2[par]
            T.dma("sync", xtm[:, :, :],
                  x_h[512 * g:512 * g + 512, :].rearrange("(c p) n -> p c n", p=128),
                  writes=["xtm"], tag=f"ldx2_{g}")
            for cc in range(4):
                for fq in range(4):
                    for ff in range(4):
                        fc = fq * 4 + ff
                        T.emit("tensor", lambda t, cc=cc, fc=fc, ff=ff: t.transpose(
                            psb(0)[:, 128 * ff:128 * (ff + 1)],
                            xtm[:, cc, 128 * fc:128 * (fc + 1)], ident),
                            reads=["xtm", "co"], writes=["psB0"])
                    T.emit("vector", lambda v, cc=cc, fq=fq, xf=xf: v.tensor_copy(
                        xf[:, 4 * fq:4 * fq + 4, 128 * cc:128 * (cc + 1)],
                        psb(0)[:, 0:512].rearrange("p (c n) -> p c n", c=4)),
                        reads=["psB0"],
                        writes=[f"xf{par}_{4 * fq + i}" for i in range(4)])

        def p3_store(gq):
            parq = gq % 2
            with T.dma_batch("gpsimd"):
                for dc in range(NDC):
                    T.dma("gpsimd", xqs[gq, dc, :, :], xfm2[parq][:, dc, :],
                          reads=[f"xf{parq}_{dc}"], writes=[f"xqsc{gq}"],
                          tag=f"stxq{gq}_{dc}")

        T.dma("sync", osg[:, :, :], os_[0, :, :, :].rearrange("h p n -> p h n"),
              reads=[f"ossc0_{hh}" for hh in range(H)], writes=["osg"], tag="ldo0")
        p3_xload(0)
        for g in range(4):
            par = g % 2
            parq = 1 - par
            xf = xfm2[par]
            for grp in range(4):
                if g > 0:
                    ladder(xfm2f[parq][:, 2048 * grp:2048 * (grp + 1)],
                           xtmf[:, 0:4096],
                           [f"xf{parq}_{4 * grp + i}" for i in range(4)],
                           2048, tmp_key="xtm")
                for dc in range(4 * grp, 4 * grp + 4):
                    wb = dc % 2
                    pa = dc % 2
                    T.dma("sync", woc[wb][:, :, :],
                          wo_h[:, 128 * dc:128 * (dc + 1)]
                          .rearrange("(c p) n -> p c n", p=128),
                          writes=[f"woc{wb}"], tag=f"ldwo{g}_{dc}")
                    for hd in range(H):
                        T.emit("tensor", lambda t, wb=wb, hd=hd, pa=pa: t.matmul(
                            psb(pa)[:, 0:512], woc[wb][:, hd, :], osg[:, hd, :],
                            start=(hd == 0), stop=(hd == H - 1)),
                            reads=[f"woc{wb}", "osg"], writes=[f"psB{pa}"],
                            tag=f"wo{g}_{dc}")
                    T.emit("vector", lambda v, dc=dc, pa=pa, xf=xf: v.tensor_add(
                        x1g[:, dc, :], psb(pa)[:, 0:512], xf[:, dc, :]),
                        reads=[f"psB{pa}", f"xf{par}_{dc}"], writes=[f"x1g_{dc}"])
                    T.emit("tensor", lambda t, dc=dc: t.matmul(
                        psb(2)[:, 0:512], ONES[:], x1g[:, dc, :],
                        start=(dc == 0), stop=(dc == NDC - 1)),
                        reads=["ones", f"x1g_{dc}"], writes=["psB2"], tag=f"lns{g}")
                    T.emit("scalar", lambda s, dc=dc, xf=xf: s.activation(
                        xf[:, dc, :], x1g[:, dc, :], AF.Square),
                        reads=[f"x1g_{dc}"], writes=[f"xf{par}_{dc}"])
                    T.emit("tensor", lambda t, dc=dc, xf=xf: t.matmul(
                        psb(3)[:, 0:512], ONES[:], xf[:, dc, :],
                        start=(dc == 0), stop=(dc == NDC - 1)),
                        reads=["ones", f"xf{par}_{dc}"], writes=["psB3"],
                        tag=f"lnsq{g}")
                    T.dma("sync", x1s[g, dc, :, :], x1g[:, dc, :],
                          reads=[f"x1g_{dc}"], writes=[f"x1sc{g}"],
                          tag=f"stx1_{g}_{dc}")
            if g > 0:
                p3_store(g - 1)
            if g < 3:
                T.dma("sync", osg[:, :, :],
                      os_[g + 1, :, :, :].rearrange("h p n -> p h n"),
                      reads=[f"ossc{g + 1}_{hh}" for hh in range(H)],
                      writes=["osg"], tag=f"ldo{g + 1}")
            T.emit("vector", lambda v: v.tensor_scalar(
                MREP[:], psb(2)[:, 0:512], 1.0 / DM, None, op0=ALU.mult),
                reads=["psB2"], writes=["mrep"])
            T.emit("vector", lambda v: v.tensor_mul(SR3[:, 0:512], MREP[:], MREP[:]),
                   reads=["mrep"], writes=["sr3a"])
            T.emit("vector", lambda v: v.scalar_tensor_tensor(
                RREP[:], psb(3)[:, 0:512], 1.0 / DM, SR3[:, 0:512],
                op0=ALU.mult, op1=ALU.subtract),
                reads=["psB3", "sr3a"], writes=["rrep"])
            if probe_ap is not None:
                T.dma("gpsimd", probe_ap[g, :, 1024:1536], RREP[:], reads=["rrep"], tag="pbv")
                T.dma("gpsimd", probe_ap[g, :, 1536:2048], SR3[:, 0:512], reads=["sr3a"], tag="pbm2")
            T.emit("scalar", lambda s: s.activation(RREP[:], RREP[:], AF.Sqrt, bias=EPSC),
                   reads=["rrep", "epsc"], writes=["rrep"])
            T.emit("vector", lambda v: v.reciprocal(SR3[:, 0:512], RREP[:]),
                   reads=["rrep"], writes=["sr3a"])
            T.emit("vector", lambda v: v.tensor_mul(SR3[:, 512:1024], RREP[:], SR3[:, 0:512]),
                   reads=["rrep", "sr3a"], writes=["p3u"])
            T.emit("vector", lambda v: v.tensor_mul(SR3[:, 1024:1536], SR3[:, 512:1024],
                                                    SR3[:, 512:1024]),
                   reads=["p3u"], writes=["p3t"])
            T.emit("vector", lambda v: v.tensor_scalar(
                SR3[:, 1024:1536], SR3[:, 1024:1536], -0.5, 1.5, op0=ALU.mult, op1=ALU.add),
                reads=["p3t"], writes=["p3t"])
            T.emit("vector", lambda v: v.tensor_mul(RREP[:], SR3[:, 0:512], SR3[:, 1024:1536]),
                   reads=["sr3a", "p3t"], writes=["rrep"])
            if probe_ap is not None:
                T.dma("gpsimd", probe_ap[g, :, 0:512], MREP[:], reads=["mrep"], tag="pb1")
                T.dma("gpsimd", probe_ap[g, :, 512:1024], RREP[:], reads=["rrep"], tag="pb2")
            T.emit("vector", lambda v: v.memset(SR3[:, 1536:2048], 0.0), writes=["p3m16"])
            T.emit("vector", lambda v: v.memset(SR3[:, 512:1024], 0.0), writes=["p3u"])
            for dc in range(NDC):
                T.emit("vector", lambda v, dc=dc, xf=xf: v.tensor_sub(
                    xf[:, dc, :], x1g[:, dc, :], MREP[:]),
                    reads=[f"x1g_{dc}", "mrep"], writes=[f"xf{par}_{dc}"])
                T.emit("vector", lambda v, dc=dc, xf=xf: v.tensor_mul(
                    xf[:, dc, :], xf[:, dc, :], RREP[:]),
                    reads=[f"xf{par}_{dc}", "rrep"], writes=[f"xf{par}_{dc}"])
                T.emit("vector", lambda v, dc=dc, xf=xf: v.tensor_max(
                    SR3[:, 1536:2048], SR3[:, 1536:2048], xf[:, dc, :]),
                    reads=[f"xf{par}_{dc}", "p3m16"], writes=["p3m16"])
                T.emit("vector", lambda v, dc=dc, xf=xf: v.tensor_tensor(
                    SR3[:, 512:1024], SR3[:, 512:1024], xf[:, dc, :], op=ALU.min),
                    reads=[f"xf{par}_{dc}", "p3u"], writes=["p3u"])
            T.emit("vector", lambda v: v.tensor_scalar(
                SR3[:, 512:1024], SR3[:, 512:1024], -1.0, None, op0=ALU.mult),
                reads=["p3u"], writes=["p3u"])
            T.emit("vector", lambda v: v.tensor_max(
                SR3[:, 1536:2048], SR3[:, 1536:2048], SR3[:, 512:1024]),
                reads=["p3u", "p3m16"], writes=["p3m16"])
            for ts in range(4):
                T.emit("tensor", lambda t, ts=ts: t.transpose(
                    psb(0)[:, 0:128], SR3[:, 1536 + 128 * ts:1536 + 128 * (ts + 1)], ident),
                    reads=["p3m16", "co"], writes=["psB0"])
                T.emit("vector", lambda v, ts=ts: v.reduce_max(
                    SM2[:, ts:ts + 1], psb(0)[:, 0:128], axis=AX.X),
                    reads=["psB0"], writes=["amax1"])
            T.emit("vector", lambda v: v.tensor_scalar(
                SM2[:, 0:4], SM2[:, 0:4], EPS, None, op0=ALU.max),
                reads=["amax1"], writes=["amax1"])
            T.emit("tensor", lambda t: t.transpose(psb(1)[:, 0:128], SM2[:, 0:128], ident),
                   reads=["amax1", "co"], writes=["psB1"])
            T.emit("vector", lambda v: v.tensor_copy(SM2[:, 128:256], psb(1)[:, 0:128]),
                   reads=["psB1"], writes=["amaxT"])
            with T.dma_batch("gpsimd"):
                for c in range(4):
                    T.dma("gpsimd", SR3[0:1, 128 * c:128 * (c + 1)], SM2[c:c + 1, 128:256],
                          reads=["amaxT"], writes=["sr3a"], tag=f"af{g}_{c}")
            T.emit("tensor", lambda t: t.matmul(
                psb(1)[:, 0:512], ONES[0:1, :], SR3[0:1, 0:512], start=True, stop=True),
                reads=["sr3a", "ones"], writes=["psB1"], tag=f"rep{g}")
            T.emit("vector", lambda v: v.tensor_scalar(
                MREP[:], psb(1)[:, 0:512], 1.0 / 127.0, None, op0=ALU.mult),
                reads=["psB1"], writes=["mrep"])
            T.emit("vector", lambda v: v.reciprocal(RREP[:], MREP[:]),
                   reads=["mrep"], writes=["rrep"])
            T.dma("sync", ss1[g, :, :], MREP[:], reads=["mrep"], writes=[f"ss1c{g}"],
                  tag=f"stss1_{g}")
            for dc in range(NDC):
                T.emit("vector", lambda v, dc=dc, xf=xf: v.tensor_mul(
                    xf[:, dc, :], xf[:, dc, :], RREP[:]),
                    reads=[f"xf{par}_{dc}", "rrep"], writes=[f"xf{par}_{dc}"])
            if g < 3:
                p3_xload(g + 1)
        for grp in range(4):
            ladder(xfm2f[1][:, 2048 * grp:2048 * (grp + 1)], xtmf[:, 0:4096],
                   [f"xf1_{4 * grp + i}" for i in range(4)], 2048, tmp_key="xtm")
        p3_store(3)

        dbg_dump(T, ["x1s", "xqs"])
        T.phase()

        AR2_ctx.__exit__(None, None, None)
        BFB_ctx = nc.sbuf_tensor("bfarena", [128, 44032], BF16)
        BFB = BFB_ctx.__enter__()
        wl4 = [AR1[:, 2048 * i:2048 * (i + 1)] for i in range(2)]
        tq4 = [AR1[:, 4096 + 2048 * i:4096 + 2048 * (i + 1)] for i in range(2)]
        wqT32g = AR1[:, 8192:10240]
        wqT32u = AR1[:, 10240:12288]
        gs_b = AR1[:, 12288:14336]
        hb = AR1[:, 14336:16384]
        M43 = AR1[:, 16384:18432]
        is1 = AR1[:, 18432:20480]
        M43N = AR1[:, 20480:22528]
        xqb = BFB[:, 0:32768].rearrange("p (c n) -> p c n", c=NDC)
        wqbg = [BFB[:, 32768 + 2048 * i:32768 + 2048 * (i + 1)] for i in range(2)]
        wqbu = [BFB[:, 36864 + 2048 * i:36864 + 2048 * (i + 1)] for i in range(2)]

        with T.dma_batch("sync"):
            for g in range(4):
                for dc in range(NDC):
                    T.dma("sync", xqb[:, dc, 512 * g:512 * (g + 1)], xqs[g, dc, :, :],
                          reads=[f"xqsc{g}"], writes=["xqb"], tag=f"ldxq{g}_{dc}")
        T.emit("vector", lambda v: v.memset(M43[:], 0.0),
               writes=["m43h0", "m43h1"])
        T.emit("vector", lambda v: v.memset(M43N[:], 0.0),
               writes=["m43n0", "m43n1"])
        with T.dma_batch("sync"):
            for g in range(4):
                T.dma("sync", is1[:, 512 * g:512 * (g + 1)], ss1[g, :, :],
                      reads=[f"ss1c{g}"], writes=["is1"], tag=f"ldss1_{g}")

        pend4 = []

        def gateup_quant(w_hh, fc, par, dbkey):
            T.dma("sync", wl4[par][:], w_hh[128 * fc:128 * (fc + 1), :],
                  writes=[f"wl4{par}"], tag=f"ld{dbkey}{fc}")
            ternary_quant(wl4[par], f"wl4{par}", tq4[par], f"tq4{par}", NDC,
                          sink=pend4)

        def gateup_finish(par, dst32, d32key, dstb, dbkey, banks):
            for i4, c0 in enumerate(range(0, 16, 4)):
                bk = banks[i4 % 2]
                for ff in range(4):
                    c = c0 + ff
                    T.emit("tensor", lambda t, c=c, ff=ff, par=par, bk=bk: t.transpose(
                        psb(bk)[:, 128 * ff:128 * (ff + 1)],
                        tq4[par][:, 128 * c:128 * (c + 1)], ident),
                        reads=[f"tq4{par}", "co"], writes=[f"psB{bk}"])
                T.emit("scalar", lambda s, c0=c0, dst32=dst32, bk=bk: s.activation(
                    dst32[:, 128 * c0:128 * c0 + 512], psb(bk)[:, 0:512], AF.Copy),
                    reads=[f"psB{bk}"], writes=[d32key])
            T.emit("vector", lambda v, dst32=dst32, dstb=dstb: v.tensor_copy(
                dstb[0][:], dst32[:]), reads=[d32key], writes=[dbkey + "h"])
            T.emit("vector", lambda v, dst32=dst32, dstb=dstb: v.tensor_sub(
                dstb[1][:], dst32[:], dstb[0][:]),
                reads=[d32key, dbkey + "h"], writes=[dbkey + "l"])

        def pop4(n=2):
            for _ in range(n):
                if pend4:
                    pend4.pop(0)()

        gateup_quant(wg_h, 0, 0, "wqbg")
        gateup_quant(wu_h, 0, 1, "wqbu")
        while pend4:
            pend4.pop(0)()
        gateup_finish(0, wqT32g, "wqT32g", wqbg, "wqbg", (0, 1))
        gateup_finish(1, wqT32u, "wqT32u", wqbu, "wqbu", (2, 3))
        for fc in range(NFC):
            if fc + 1 < NFC:
                gateup_quant(wg_h, fc + 1, 0, "wqbg")
                gateup_quant(wu_h, fc + 1, 1, "wqbu")
            for nh in range(2):
                for sl in range(2):
                    ns = 2 * nh + sl
                    for din in range(NDC):
                        for hl in range(2):
                            T.emit("tensor", lambda t, sl=sl, ns=ns, din=din, hl=hl: t.matmul(
                                PSA[:, 512 * sl:512 * (sl + 1)],
                                wqbg[hl][:, 128 * din:128 * (din + 1)],
                                xqb[:, din, 512 * ns:512 * (ns + 1)],
                                start=(din == 0 and hl == 0),
                                stop=(din == NDC - 1 and hl == 1)),
                                reads=["wqbgh", "wqbgl", "xqb"], writes=["psAg"],
                                tag=f"gmm{fc}")
                for sl in range(2):
                    ns = 2 * nh + sl
                    for din in range(NDC):
                        for hl in range(2):
                            T.emit("tensor", lambda t, sl=sl, ns=ns, din=din, hl=hl: t.matmul(
                                PSA[:, 1024 + 512 * sl:1024 + 512 * (sl + 1)],
                                wqbu[hl][:, 128 * din:128 * (din + 1)],
                                xqb[:, din, 512 * ns:512 * (ns + 1)],
                                start=(din == 0 and hl == 0),
                                stop=(din == NDC - 1 and hl == 1)),
                                reads=["wqbuh", "wqbul", "xqb"], writes=["psAu"],
                                tag=f"umm{fc}")
                h0 = 1024 * nh
                T.emit("vector", lambda v, h0=h0: v.tensor_mul(
                    gs_b[:, h0:h0 + 1024], PSA[:, 0:1024], is1[:, h0:h0 + 1024]),
                    reads=["psAg", "is1"], writes=[f"gs{nh}"])
                pop4()
                T.emit("scalar", lambda s, h0=h0: s.activation(
                    gs_b[:, h0:h0 + 1024], gs_b[:, h0:h0 + 1024], AF.Silu),
                    reads=[f"gs{nh}"], writes=[f"gs{nh}"])
                T.emit("vector", lambda v, h0=h0: v.tensor_mul(
                    hb[:, h0:h0 + 1024], PSA[:, 1024:2048], is1[:, h0:h0 + 1024]),
                    reads=["psAu", "is1"], writes=[f"hb{nh}"])
                pop4()
                T.emit("vector", lambda v, h0=h0: v.tensor_mul(
                    hb[:, h0:h0 + 1024], gs_b[:, h0:h0 + 1024], hb[:, h0:h0 + 1024]),
                    reads=[f"gs{nh}", f"hb{nh}"], writes=[f"hb{nh}"])
                pop4()
                T.emit("vector", lambda v, h0=h0: v.tensor_max(
                    M43[:, h0:h0 + 1024], M43[:, h0:h0 + 1024], hb[:, h0:h0 + 1024]),
                    reads=[f"hb{nh}", f"m43h{nh}"], writes=[f"m43h{nh}"])
                pop4()
                T.emit("vector", lambda v, h0=h0: v.tensor_tensor(
                    M43N[:, h0:h0 + 1024], M43N[:, h0:h0 + 1024],
                    hb[:, h0:h0 + 1024], op=ALU.min),
                    reads=[f"hb{nh}", f"m43n{nh}"], writes=[f"m43n{nh}"])
                pop4()
            while pend4:
                pend4.pop(0)()
            if fc + 1 < NFC:
                gateup_finish(0, wqT32g, "wqT32g", wqbg, "wqbg", (0, 1))
                gateup_finish(1, wqT32u, "wqT32u", wqbu, "wqbu", (2, 3))
            T.dma("sync", hs[fc, :, :], hb[:], reads=["hb0", "hb1"],
                  writes=[f"hsc{fc}"], tag=f"sth{fc}")

        dbg_dump(T, ["hs"])

        T.emit("vector", lambda v: v.tensor_scalar(
            M43N[:], M43N[:], -1.0, None, op0=ALU.mult),
            reads=["m43n0", "m43n1"], writes=["m43n0", "m43n1"])
        T.emit("vector", lambda v: v.tensor_max(M43[:], M43[:], M43N[:]),
               reads=["m43n0", "m43n1", "m43h0", "m43h1"],
               writes=["m43h0", "m43h1"])
        for ts in range(16):
            T.emit("tensor", lambda t, ts=ts: t.transpose(
                psb(0)[:, 0:128], M43[:, 128 * ts:128 * (ts + 1)], ident),
                reads=["m43h0", "m43h1", "co"], writes=["psB0"])
            T.emit("vector", lambda v, ts=ts: v.reduce_max(
                SM2[:, ts:ts + 1], psb(0)[:, 0:128], axis=AX.X),
                reads=["psB0"], writes=["amax2", "q_s0", "q_sc"])
        T.emit("vector", lambda v: v.tensor_scalar(
            SM2[:, 0:16], SM2[:, 0:16], EPS, None, op0=ALU.max),
            reads=["amax2"], writes=["amax2"])
        T.emit("tensor", lambda t: t.transpose(psb(1)[:, 0:128], SM2[:, 0:128], ident),
               reads=["amax2", "co"], writes=["psB1"])
        T.emit("vector", lambda v: v.tensor_copy(SM2[:, 128:256], psb(1)[:, 0:128]),
               reads=["psB1"], writes=["amax2T", "q_thr", "q_nthr"])
        T.phase()

        sstg4 = [AR1[:, 2048 * i:2048 * (i + 1)] for i in range(2)]
        ltmp = AR1[:, 4096:8192]
        outtm = AR1[:, 8192:16384]
        outtm3 = outtm.rearrange("p (c n) -> p c n", c=4)
        s2rep = AR1[:, 16384:18432]
        is2rep = AR1[:, 18432:20480]
        hld = [AR1[:, 20480 + 512 * i:20480 + 512 * (i + 1)] for i in range(2)]
        x1l = [AR1[:, 21504 + 512 * i:21504 + 512 * (i + 1)] for i in range(2)]
        otmp2 = [AR1[:, 22528 + 512 * i:22528 + 512 * (i + 1)] for i in range(2)]
        hqb3 = BFB[:, 0:22016].rearrange("p (c n) -> p c n", c=NFC)
        wq4 = [[BFB[:, 22016 + 8192 * s2_ + 1024 * l_:
                    22016 + 8192 * s2_ + 1024 * (l_ + 1)]
                for l_ in range(8)] for s2_ in range(2)]

        def lanebank(l):
            return PSA[:, 512 * l:512 * (l + 1)] if l < 4 else \
                PSB[:, 512 * (l - 4):512 * (l - 3)]

        def lkey(l):
            return f"psA{l}" if l < 4 else f"psB{l - 4}"

        with T.dma_batch("gpsimd"):
            for c in range(16):
                T.dma("gpsimd", s2rep[0:1, 128 * c:128 * (c + 1)], SM2[c:c + 1, 128:256],
                      reads=["amax2T"], writes=["amax2F"], tag=f"af2_{c}")
        for ns in range(4):
            T.emit("tensor", lambda t, ns=ns: t.matmul(
                PSA[:, 512 * ns:512 * (ns + 1)], ONES[0:1, :],
                s2rep[0:1, 512 * ns:512 * (ns + 1)], start=True, stop=True),
                reads=["amax2F", "ones"], writes=[f"psA{ns}"], tag=f"rep2_{ns}")
        T.emit("vector", lambda v: v.tensor_scalar(
            is2rep[:], PSA[:], 1.0 / 127.0, None, op0=ALU.mult),
            reads=["psA0", "psA1", "psA2", "psA3"], writes=["is2"])
        T.emit("vector", lambda v: v.reciprocal(s2rep[:], is2rep[:]),
               reads=["is2"], writes=["s2"])

        NCH = (NFC + 3) // 4

        def chdim(ch):
            fc0 = 4 * ch
            return fc0, min(4, NFC - fc0)

        def emit_ladder(g, ch):
            fc0, fcn = chdim(ch)
            par = ch % 2
            sstg = sstg4[par]
            for i in range(fcn):
                fc = fc0 + i
                hb2 = i % 2
                T.dma("sync", hld[hb2][:], hs[fc, :, 512 * g:512 * (g + 1)],
                      reads=[f"hsc{fc}"], writes=[f"hld{hb2}"], tag=f"ldh{g}_{fc}")
                T.emit("vector", lambda v, i=i, hb2=hb2, g=g, sstg=sstg: v.tensor_mul(
                    sstg[:, 512 * i:512 * (i + 1)], hld[hb2][:],
                    s2rep[:, 512 * g:512 * (g + 1)]),
                    reads=[f"hld{hb2}", "s2"], writes=[f"sstg{par}"])
            width = 512 * fcn
            ladder(sstg[:, 0:width], ltmp[:, 0:2 * width], f"sstg{par}", width,
                   tmp_key="ltmp", out=BFB[:, 512 * fc0:512 * fc0 + width],
                   out_key=f"hqc{ch}")

        def emit_half(g, ch, dc0):
            fc0, fcn = chdim(ch)
            par = ch % 2
            with T.dma_batch("sync"):
                for l in range(8):
                    dc = dc0 + l
                    for hl in range(2):
                        T.dma("sync", wq4[par][l][:, fcn * 128 * hl:fcn * 128 * (hl + 1)]
                              .rearrange("p (c n) -> p c n", c=fcn),
                              wds[dc, hl, fc0:fc0 + fcn, :, :]
                              .rearrange("c p n -> p c n"),
                              reads=[f"wdsc{dc}"], writes=[f"wq{par}_{l}"],
                              tag=f"ldw_{g}_{ch}_{dc}_{hl}")
            for l in range(8):
                for i in range(fcn):
                    for hl in range(2):
                        off = 128 * (fcn * hl + i)
                        T.emit("tensor", lambda t, l=l, par=par, off=off, i=i,
                               hl=hl, fc0=fc0, ch=ch, fcn=fcn: t.matmul(
                            lanebank(l), wq4[par][l][:, off:off + 128],
                            hqb3[:, fc0 + i, :],
                            start=(ch == 0 and i == 0 and hl == 0),
                            stop=(ch == NCH - 1 and i == fcn - 1 and hl == 1)),
                            reads=[f"wq{par}_{l}", f"hqc{ch}"], writes=[lkey(l)],
                            tag=f"dm_{g}_{dc0 + l}")

        def evac_lane(g, dc, l):
            wb = l % 2
            T.dma("sync", x1l[wb][:], x1s[g, dc, :, :], reads=[f"x1sc{g}"],
                  writes=[f"x1l{wb}"], tag=f"ldx1{g}_{dc}")
            T.emit("vector", lambda v, l=l, g=g, wb=wb: v.tensor_mul(
                otmp2[wb][:], lanebank(l)[:, 0:512],
                is2rep[:, 512 * g:512 * (g + 1)]),
                reads=[lkey(l), "is2"], writes=[f"otmp{wb}"])
            T.emit("vector", lambda v, wb=wb: v.tensor_add(
                otmp2[wb][:], otmp2[wb][:], x1l[wb][:]),
                reads=[f"otmp{wb}", f"x1l{wb}"], writes=[f"otmp{wb}"])
            for ff in range(4):
                T.emit("tensor", lambda t, l=l, wb=wb, ff=ff: t.transpose(
                    lanebank(l)[:, 128 * ff:128 * (ff + 1)],
                    otmp2[wb][:, 128 * ff:128 * (ff + 1)], ident),
                    reads=[f"otmp{wb}", "co"], writes=[lkey(l)])
            T.emit("scalar", lambda s, l=l, dc=dc: s.activation(
                outtm3[:, 0:4, 128 * dc:128 * (dc + 1)],
                lanebank(l)[:, 0:512].rearrange("p (c n) -> p c n", c=4), AF.Copy),
                reads=[lkey(l)], writes=["outtm"])

        def store_out(g):
            tok0 = 512 * g
            T.dma("sync", out_h[tok0:tok0 + 512, :].rearrange("(c p) n -> p c n", p=128),
                  outtm3[:, :, :], reads=["outtm"], writes=["out_h"], tag=f"stout{g}")

        for ch in range(5):
            emit_ladder(0, ch)
        for g in range(4):
            if g > 0:
                for l in range(8):
                    evac_lane(g - 1, 8 + l, l)
                store_out(g - 1)
            for ch in range(NCH):
                if ch >= 5:
                    emit_ladder(g, ch)
                emit_half(g, ch, 0)
            for l in range(8):
                evac_lane(g, l, l)
            for ch in range(NCH):
                emit_half(g, ch, 8)
                if g < 3 and ch <= 4:
                    emit_ladder(g + 1, ch)
        for l in range(8):
            evac_lane(3, 8 + l, l)
        store_out(3)

        T.barrier("sync")
        T.build()
        BFB_ctx.__exit__(None, None, None)
    return nc


def kernel(L0_attn_hidden_states, L0_attn_Wq, L0_attn_Wk, L0_attn_Wv, L0_attn_Wo,
           L0_ffn_w_gate, L0_ffn_w_up, L0_ffn_w_down, _dbg=(), _cores=8):
    x = np.ascontiguousarray(np.asarray(L0_attn_hidden_states, dtype=np.float32))
    consts = host_consts()
    wq = np.ascontiguousarray(np.asarray(L0_attn_Wq, np.float32))
    wk = np.ascontiguousarray(np.asarray(L0_attn_Wk, np.float32))
    wv = np.ascontiguousarray(np.asarray(L0_attn_Wv, np.float32))
    wo = np.ascontiguousarray(np.asarray(L0_attn_Wo, np.float32))
    wg = np.ascontiguousarray(np.asarray(L0_ffn_w_gate, np.float32))
    wu = np.ascontiguousarray(np.asarray(L0_ffn_w_up, np.float32))
    wd = np.ascontiguousarray(np.asarray(L0_ffn_w_down, np.float32))
    nc = build_program(dbg=_dbg)
    in_maps = [{"x": np.ascontiguousarray(x[b]), "wq": wq, "wk": wk, "wv": wv,
                "wo": wo, "wg": wg, "wu": wu, "wd": wd, "consts": consts}
               for b in range(_cores)]
    res = run_bass_kernel_spmd(nc, in_maps, core_ids=list(range(_cores)))
    out = np.stack([res.results[b]["out"] for b in range(_cores)], axis=0)
    if _dbg:
        return out, res
    return out


# version 15
# speedup vs baseline: 1.0290x; 1.0290x over previous
import numpy as np
from contextlib import ExitStack, contextmanager

import concourse.bass as bass
import concourse.mybir as mybir
from concourse.bass_utils import run_bass_kernel_spmd

F32 = mybir.dt.float32
BF16 = mybir.dt.bfloat16
AF = mybir.ActivationFunctionType
ALU = mybir.AluOpType
AX = mybir.AxisListType

H = 16
D = 128
S = 2048
DM = 2048
DFF = 5504
NDC = DM // 128
NFC = DFF // 128
NB = 32
SM = 128 ** -0.5
BIG = 1.0e30
EPS = 1e-5

ENGINES = ("tensor", "vector", "scalar", "gpsimd", "sync")
COMPUTE_SELFSYNC = ("vector", "scalar", "gpsimd")


class Tracker:
    def __init__(self, nc, dma_depth=4):
        self.nc = nc
        self.ops = {e: [] for e in ENGINES}
        self.cnt = {e: 0 for e in ENGINES}
        self.dma_depth = dma_depth
        self.dq = {q: {"batch": 0, "targets": [0] * dma_depth}
                   for q in ("sync", "scalar", "gpsimd")}
        self.last_write = {}
        self.readers = {}
        self.watermark = {e: {} for e in ENGINES}
        self._batch_ctx = None
        self.n_waits = 0

    def _label_val(self, label):
        if label.startswith("E:"):
            return self.cnt[label[2:]]
        _, q, s = label.split(":")
        return self.dq[q]["targets"][int(s)]

    def _edges(self, eng, reads, writes, self_label):
        waits = {}

        def need(label, val):
            if label == self_label and not (eng in COMPUTE_SELFSYNC and label.startswith("E:")):
                return
            if label == "E:tensor" and eng == "tensor":
                return
            if self.watermark[eng].get(label, 0) >= val:
                return
            if waits.get(label, 0) < val:
                waits[label] = val

        for k in reads:
            w = self.last_write.get(k)
            if w:
                need(*w)
        for k in writes:
            w = self.last_write.get(k)
            if w:
                need(*w)
            for label, val in self.readers.get(k, {}).items():
                need(label, val)
        for label, val in waits.items():
            self.watermark[eng][label] = val
        self.n_waits += len(waits)
        return waits

    def _record(self, label, reads, writes):
        v = self._label_val(label)
        for k in writes:
            self.last_write[k] = (label, v)
            self.readers[k] = {}
        for k in reads:
            self.readers.setdefault(k, {})[label] = v

    def emit(self, eng, fn, reads=(), writes=(), tag=None):
        self_label = f"E:{eng}"
        waits = self._edges(eng, reads, writes, self_label)
        self.cnt[eng] += 1
        self.ops[eng].append((waits, fn, self_label, 1, tag))
        self._record(self_label, reads, writes)
        return self.cnt[eng]

    @contextmanager
    def dma_batch(self, queue):
        assert self._batch_ctx is None
        q = self.dq[queue]
        s = q["batch"] % self.dma_depth
        q["batch"] += 1
        gate = q["targets"][s]
        self._batch_ctx = (queue, s, [], [], gate)
        try:
            yield
        finally:
            queue_, s_, rd, wr, _ = self._batch_ctx
            self._batch_ctx = None
            self._record(f"D:{queue_}:{s_}", rd, wr)

    def dma(self, queue, out, in_, reads=(), writes=(), tag=None, **kw):
        standalone = self._batch_ctx is None
        if standalone:
            ctx = self.dma_batch(queue)
            ctx.__enter__()
        queue_, s, rd, wr, gate = self._batch_ctx
        assert queue_ == queue
        label = f"D:{queue}:{s}"
        waits = self._edges(queue, reads, writes, label)
        if gate > 0 and self.watermark[queue].get(label, 0) < gate:
            waits[label] = max(waits.get(label, 0), gate)
            self.watermark[queue][label] = gate
        self.dq[queue]["targets"][s] += 16
        self.ops[queue].append(
            (waits, lambda h, o=out, i=in_, kw=kw: h.dma_start(out=o, in_=i, **kw),
             label, 16, tag))
        rd.extend(reads)
        wr.extend(writes)
        if standalone:
            ctx.__exit__(None, None, None)

    def barrier(self, eng, labels=None):
        if labels is None:
            labels = {}
            for e in ENGINES:
                if e != eng and self.cnt[e]:
                    labels[f"E:{e}"] = self.cnt[e]
            for qname, q in self.dq.items():
                for s, t in enumerate(q["targets"]):
                    if t:
                        labels[f"D:{qname}:{s}"] = t
        waits = {}
        for label, val in labels.items():
            if self.watermark[eng].get(label, 0) < val:
                waits[label] = val
                self.watermark[eng][label] = val
        if waits:
            self.ops[eng].append((waits, None, None, 0, "barrier"))

    def phase(self):
        for e in ENGINES:
            self.barrier(e)

    def build(self):
        nc = self.nc
        labels = set()
        for e in ENGINES:
            for waits, _, inc_label, _, _ in self.ops[e]:
                labels.update(waits)
                if inc_label:
                    labels.add(inc_label)
        with ExitStack() as stack:
            sems = {lb: stack.enter_context(nc.semaphore(lb.replace(":", "_")))
                    for lb in sorted(labels)}
            block = stack.enter_context(nc.Block())
            for eng in ENGINES:
                if not self.ops[eng]:
                    continue

                def make(e, ops_e):
                    def run(handle):
                        for waits, fn, inc_label, inc_amt, tag in ops_e:
                            for lb, v in sorted(waits.items()):
                                handle.wait_ge(sems[lb], v)
                            if fn is not None:
                                ins = fn(handle)
                                if inc_label is not None:
                                    ins.then_inc(sems[inc_label], inc_amt)
                    return run

                getattr(block, eng)(make(eng, self.ops[eng]))
        return sems


def host_consts():
    ext = np.zeros((128, 64), np.float32)
    for p in range(128):
        t_self = 32 + (p // 64)
        ext[p, t_self] = BIG
        ext[p, t_self + 1:] = -BIG
    causal = np.zeros((128, 128), np.float32)
    for q in range(128):
        causal[q, q + 1:] = -BIG
    ident = np.eye(128, dtype=np.float32)
    return np.concatenate([ext, causal, ident], axis=1)


def build_program(dbg=()):
    nc = bass.Bass("TRN2", target_bir_lowering=False)
    x_h = nc.dram_tensor("x", [S, DM], F32, kind="ExternalInput").ap()
    wq_h = nc.dram_tensor("wq", [DM, DM], F32, kind="ExternalInput").ap()
    wk_h = nc.dram_tensor("wk", [DM, DM], F32, kind="ExternalInput").ap()
    wv_h = nc.dram_tensor("wv", [DM, DM], F32, kind="ExternalInput").ap()
    wo_h = nc.dram_tensor("wo", [DM, DM], F32, kind="ExternalInput").ap()
    wg_h = nc.dram_tensor("wg", [DFF, DM], F32, kind="ExternalInput").ap()
    wu_h = nc.dram_tensor("wu", [DFF, DM], F32, kind="ExternalInput").ap()
    wd_h = nc.dram_tensor("wd", [DM, DFF], F32, kind="ExternalInput").ap()
    co_h = nc.dram_tensor("consts", [128, 320], F32, kind="ExternalInput").ap()
    out_h = nc.dram_tensor("out", [S, DM], F32, kind="ExternalOutput").ap()

    qs = nc.dram_tensor("qs", [H, 128, S], F32, kind="Internal").ap()
    ks = nc.dram_tensor("ks", [H, 128, S], F32, kind="Internal").ap()
    vs = nc.dram_tensor("vs", [H, 128, S], F32, kind="Internal").ap()
    os_ = nc.dram_tensor("os", [4, H, 128, 512], F32, kind="Internal").ap()
    x1s = nc.dram_tensor("x1s", [4, NDC, 128, 512], F32, kind="Internal").ap()
    xqs = nc.dram_tensor("xqs", [4, NDC, 128, 512], BF16, kind="Internal").ap()
    hs = nc.dram_tensor("hs", [NFC, 128, S], F32, kind="Internal").ap()
    wds = nc.dram_tensor("wds", [NDC, 2, NFC, 128, 128], BF16, kind="Internal").ap()
    ss1 = nc.dram_tensor("ss1", [4, 128, 512], F32, kind="Internal").ap()

    dbg_shapes = {"qs": [H, 128, S], "ks": [H, 128, S], "vs": [H, 128, S],
                  "os": [4, H, 128, 512], "x1s": [4, NDC, 128, 512],
                  "xqs": [4, NDC, 128, 512], "hs": [NFC, 128, S],
                  "probe": [4, 128, 2048]}
    dbg_src = {"qs": qs, "ks": ks, "vs": vs, "os": os_, "x1s": x1s,
               "xqs": xqs, "hs": hs, "wds": wds}
    dbg_aps = {n: nc.dram_tensor("dbg_" + n, dbg_shapes[n], F32,
                                 kind="ExternalOutput").ap() for n in dbg}
    probe_ap = dbg_aps.get("probe")

    def dbg_dump(T, names):
        for n in names:
            if n in dbg_aps:
                T.barrier("gpsimd")
                T.dma("gpsimd", dbg_aps[n][:], dbg_src[n][:], tag="dbg_" + n)

    AW = 49152
    with ExitStack() as st:
        e = st.enter_context
        AR1 = e(nc.sbuf_tensor("arena1", [128, 24576], F32))
        WDTBH = e(nc.sbuf_tensor("wdtbh", [128, 2048], BF16))
        WDTBL = e(nc.sbuf_tensor("wdtbl", [128, 2048], BF16))
        CO = e(nc.sbuf_tensor("co", [128, 320], F32))
        KG = e(nc.sbuf_tensor("kg", [128, H * NB], F32))
        GW = e(nc.sbuf_tensor("gw", [128, 512], F32))
        T8 = e(nc.sbuf_tensor("t8", [128, 128], F32))
        BB = e(nc.sbuf_tensor("bb", [128, 512], F32))
        SMALL = e(nc.sbuf_tensor("small", [128, 160], F32))
        SM2 = e(nc.sbuf_tensor("small2", [128, 256], F32))
        ONES = e(nc.sbuf_tensor("ones", [128, 128], F32))
        MREP = e(nc.sbuf_tensor("mrep", [128, 512], F32))
        RREP = e(nc.sbuf_tensor("rrep", [128, 512], F32))
        PSA = e(nc.psum_tensor("psA", [128, 2048], F32))
        PSB = e(nc.psum_tensor("psB", [128, 2048], F32))

        ident = CO[:, 192:320]
        causal = CO[:, 64:192]

        def psb(i):
            return PSB[:, 512 * i:512 * (i + 1)]

        T = Tracker(nc)
        T.dma("sync", CO[:], co_h[:, :], writes=["co"])
        T.emit("vector", lambda v: v.memset(ONES[:], 1.0), writes=["ones"])
        T.emit("vector", lambda v: v.memset(SM2[:], 0.0),
               writes=["amax1", "amax2", "q_s0", "q_sc", "q_thr", "q_nthr",
                       "amaxT", "amax2T"])
        T.emit("vector", lambda v: v.memset(SMALL[:], 0.0),
               writes=["epsc", "s_sum", "s_sq", "s_m", "s_m2", "s_v", "s_sq2", "s_r0",
                       "s_rr", "s_hv", "s_t", "s_r1", "a_m0", "a_m1", "a_m2",
                       "a_sig0", "a_sig1", "a_sig2", "a_inv0", "a_inv1", "a_inv2"])
        EPSC = SMALL[:, 159:160]
        T.emit("vector", lambda v: v.memset(EPSC, EPS), writes=["epsc"])

        def ladder(buf, tmp, key_buf, width, tmp_key, out=None, out_key=None):
            z0 = tmp[:, 0:width]
            tt = tmp[:, width:2 * width]
            kb = [key_buf] if isinstance(key_buf, str) else list(key_buf)
            tk = tmp_key
            T.emit("vector", lambda v: v.tensor_copy(z0, buf),
                   reads=kb, writes=[tk])
            T.emit("vector", lambda v: v.tensor_scalar(tt, buf, 127.5, 256.0,
                                                       op0=ALU.is_ge, op1=ALU.mult),
                   reads=kb, writes=[tk])
            T.emit("vector", lambda v: v.scalar_tensor_tensor(
                buf, buf, 128.5, tt, op0=ALU.add, op1=ALU.subtract),
                reads=kb + [tk], writes=kb)
            for c in (128.0, 64.0, 32.0, 16.0, 8.0, 4.0, 2.0, 1.0):
                T.emit("vector", lambda v, c=c: v.tensor_scalar(tt, buf, c, c,
                                                                op0=ALU.is_ge, op1=ALU.mult),
                       reads=kb, writes=[tk])
                T.emit("vector", lambda v: v.tensor_sub(buf, buf, tt),
                       reads=kb + [tk], writes=kb)
            dst = buf if out is None else out
            dk = kb if out_key is None else [out_key]
            T.emit("vector", lambda v: v.scalar_tensor_tensor(
                dst, z0, 0.5, buf, op0=ALU.add, op1=ALU.subtract),
                reads=kb + [tk], writes=dk)

        def ternary_quant(wbuf, key_w, tq_buf, key_tq, ngrp, sink=None):
            if sink is None:
                emit = T.emit
            else:
                emit = lambda *a, **kw: sink.append(lambda a=a, kw=kw: T.emit(*a, **kw))
            w3 = wbuf.rearrange("p (g w) -> p g w", w=128)
            emit("vector", lambda v: v.reduce_sum(SM2[:, 0:ngrp], w3, axis=AX.X,
                                                  apply_absolute_value=True),
                 reads=[key_w], writes=["q_s0"])
            emit("vector", lambda v: v.tensor_scalar(SM2[:, 64:64 + ngrp], SM2[:, 0:ngrp],
                                                     1.0 / 128, EPS,
                                                     op0=ALU.mult, op1=ALU.add),
                 reads=["q_s0"], writes=["q_sc"])
            emit("vector", lambda v: v.tensor_scalar(SM2[:, 128:128 + ngrp],
                                                     SM2[:, 64:64 + ngrp], 0.5, None,
                                                     op0=ALU.mult),
                 reads=["q_sc"], writes=["q_thr"])
            emit("vector", lambda v: v.tensor_scalar(SM2[:, 192:192 + ngrp],
                                                     SM2[:, 128:128 + ngrp], -1.0, None,
                                                     op0=ALU.mult),
                 reads=["q_thr"], writes=["q_nthr"])
            t3 = tq_buf.rearrange("p (g w) -> p g w", w=128)
            emit("vector", lambda v: v.tensor_tensor(
                t3, w3, SM2[:, 128:128 + ngrp].rearrange("p (g o) -> p g o", o=1)
                .broadcast_to([128, ngrp, 128]), op=ALU.is_ge),
                reads=[key_w, "q_thr"], writes=[key_tq])
            emit("vector", lambda v: v.tensor_tensor(
                w3, w3, SM2[:, 192:192 + ngrp].rearrange("p (g o) -> p g o", o=1)
                .broadcast_to([128, ngrp, 128]), op=ALU.is_le),
                reads=[key_w, "q_nthr"], writes=[key_w])
            emit("vector", lambda v: v.tensor_sub(tq_buf, tq_buf, wbuf),
                 reads=[key_tq, key_w], writes=[key_tq])
            emit("vector", lambda v: v.tensor_tensor(
                t3, t3, SM2[:, 64:64 + ngrp].rearrange("p (g o) -> p g o", o=1)
                .broadcast_to([128, ngrp, 128]), op=ALU.mult),
                reads=[key_tq, "q_sc"], writes=[key_tq])

        def hilo_split(tq_buf, key_tq, hi_buf, key_hi, lo_buf, key_lo, ngrp):
            n = ngrp * 128
            T.emit("vector", lambda v: v.tensor_copy(hi_buf[:, 0:n], tq_buf[:, 0:n]),
                   reads=[key_tq], writes=[key_hi])
            T.emit("vector", lambda v: v.tensor_sub(lo_buf[:, 0:n], tq_buf[:, 0:n],
                                                    hi_buf[:, 0:n]),
                   reads=[key_tq, key_hi], writes=[key_lo])

        AR2_ctx = nc.sbuf_tensor("arena2", [128, 22528], F32)
        AR2 = AR2_ctx.__enter__()
        ych = [(AR1[:, 2048 * c:2048 * (c + 1)] if c < 12
                else AR2[:, 2048 * (c - 12):2048 * (c - 11)]) for c in range(NDC)]
        yq4 = [(AR1[:, 8192 * q:8192 * (q + 1)] if q < 3 else AR2[:, 0:8192])
               .rearrange("p (c n) -> p c n", c=4) for q in range(4)]
        xt = AR2[:, 8192:16384].rearrange("p (c n) -> p c n", c=4)
        stg = [AR2[:, 16384 + 2048 * i:16384 + 2048 * (i + 1)] for i in range(2)]

        for ti in range(4):
            tok0 = 512 * ti
            T.dma("sync", xt[:, :, :],
                  x_h[tok0:tok0 + 512, :].rearrange("(c p) n -> p c n", p=128),
                  writes=["xt"], tag=f"ldx{ti}")
            for cc in range(4):
                xc = xt[:, cc, :]
                sb = cc % 2
                T.emit("vector", lambda v, xc=xc: v.reduce_sum(
                    SMALL[:, 0:1], xc, axis=AX.X), reads=["xt"], writes=["s_sum"])
                T.emit("scalar", lambda s, xc=xc, sb=sb: s.activation(
                    stg[sb][:], xc, AF.Square, accum_out=SMALL[:, 1:2]),
                    reads=["xt"], writes=[f"stg{sb}", "s_sq"])
                T.emit("vector", lambda v: v.tensor_scalar(
                    SMALL[:, 2:3], SMALL[:, 0:1], 1.0 / DM, None, op0=ALU.mult),
                    reads=["s_sum"], writes=["s_m"])
                T.emit("vector", lambda v: v.tensor_scalar(
                    SMALL[:, 3:4], SMALL[:, 2:3], SMALL[:, 2:3], None, op0=ALU.mult),
                    reads=["s_m"], writes=["s_m2"])
                T.emit("vector", lambda v: v.tensor_scalar(
                    SMALL[:, 4:5], SMALL[:, 1:2], 1.0 / DM, SMALL[:, 3:4],
                    op0=ALU.mult, op1=ALU.subtract),
                    reads=["s_sq", "s_m2"], writes=["s_v"])
                T.emit("scalar", lambda s: s.activation(
                    SMALL[:, 5:6], SMALL[:, 4:5], AF.Sqrt, bias=EPSC),
                    reads=["s_v", "epsc"], writes=["s_sq2"])
                T.emit("vector", lambda v: v.reciprocal(SMALL[:, 6:7], SMALL[:, 5:6]),
                       reads=["s_sq2"], writes=["s_r0"])
                T.emit("vector", lambda v: v.tensor_scalar(
                    SMALL[:, 7:8], SMALL[:, 6:7], SMALL[:, 6:7], None, op0=ALU.mult),
                    reads=["s_r0"], writes=["s_rr"])
                T.emit("vector", lambda v: v.tensor_scalar(
                    SMALL[:, 8:9], SMALL[:, 4:5], 0.5, 0.5 * EPS,
                    op0=ALU.mult, op1=ALU.add), reads=["s_v"], writes=["s_hv"])
                T.emit("vector", lambda v: v.tensor_scalar(
                    SMALL[:, 9:10], SMALL[:, 7:8], SMALL[:, 8:9], 1.5,
                    op0=ALU.mult, op1=ALU.subtract), reads=["s_rr", "s_hv"],
                    writes=["s_t"])
                T.emit("vector", lambda v: v.tensor_scalar(
                    SMALL[:, 9:10], SMALL[:, 9:10], -1.0, None, op0=ALU.mult),
                    reads=["s_t"], writes=["s_t"])
                T.emit("vector", lambda v: v.tensor_scalar(
                    SMALL[:, 10:11], SMALL[:, 6:7], SMALL[:, 9:10], None, op0=ALU.mult),
                    reads=["s_r0", "s_t"], writes=["s_r1"])
                T.emit("vector", lambda v, xc=xc: v.tensor_scalar(
                    xc, xc, SMALL[:, 2:3], SMALL[:, 10:11],
                    op0=ALU.subtract, op1=ALU.mult),
                    reads=["xt", "s_m", "s_r1"], writes=["xt"])
                tok = tok0 + cc * 128
                for fq in range(4):
                    for ff in range(4):
                        fc = fq * 4 + ff
                        T.emit("tensor", lambda t, xc=xc, fc=fc, ff=ff: t.transpose(
                            psb(0)[:, 128 * ff:128 * (ff + 1)],
                            xc[:, 128 * fc:128 * (fc + 1)], ident),
                            reads=["xt", "co"], writes=["psB0"])
                    T.emit("vector", lambda v, fq=fq, tok=tok: v.tensor_copy(
                        yq4[fq][:, 0:4, tok:tok + 128],
                        psb(0)[:, 0:512].rearrange("p (c n) -> p c n", c=4)),
                        reads=["psB0"], writes=["y"])

        wbuf = [AR2[:, 8192 + 4096 * i:8192 + 4096 * (i + 1)]
                .rearrange("p (c n) -> p c n", c=NDC) for i in range(2)]
        for wi, (w_hh, scr, nm) in enumerate(((wq_h, qs, "q"), (wk_h, ks, "k"),
                                              (wv_h, vs, "v"))):
            for dh in range(8):
                bi = (wi * 8 + dh) % 2
                al = ["xt"] if (wi == 0 and dh < 2) else []
                T.dma("sync", wbuf[bi][:, :, :],
                      w_hh[:, 256 * dh:256 * (dh + 1)].rearrange("(c p) n -> p c n", p=128),
                      writes=[f"wb{bi}"] + al, tag=f"ldw{nm}{dh}")
                for sub in range(2):
                    dc = dh * 2 + sub
                    for ns in range(4):
                        for din in range(NDC):
                            T.emit("tensor", lambda t, bi=bi, sub=sub, ns=ns, din=din: t.matmul(
                                PSA[:, 512 * ns:512 * (ns + 1)],
                                wbuf[bi][:, din, 128 * sub:128 * (sub + 1)],
                                ych[din][:, 512 * ns:512 * (ns + 1)],
                                start=(din == 0), stop=(din == NDC - 1)),
                                reads=[f"wb{bi}", "y"], writes=["psA"], tag=f"mm{nm}{dc}")
                    sb = dc % 2
                    if nm == "q":
                        T.emit("vector", lambda v, sb=sb: v.tensor_scalar(
                            stg[sb][:], PSA[:], SM, None, op0=ALU.mult),
                            reads=["psA"], writes=[f"stg{sb}"])
                    else:
                        T.emit("vector", lambda v, sb=sb: v.tensor_copy(stg[sb][:], PSA[:]),
                               reads=["psA"], writes=[f"stg{sb}"])
                    if nm == "k":
                        T.emit("vector", lambda v, sb=sb, dc=dc: v.reduce_sum(
                            KG[:, NB * dc:NB * (dc + 1)],
                            stg[sb][:].rearrange("p (b w) -> p b w", w=64), axis=AX.X),
                            reads=[f"stg{sb}"], writes=["kg"])
                    T.dma("sync", scr[dc, :, :], stg[sb][:], reads=[f"stg{sb}"],
                          writes=[f"{nm}sc{dc}"], tag=f"st{nm}{dc}")

        dbg_dump(T, ["qs", "ks", "vs"])
        T.phase()

        qh = AR1[:, 0:2048]
        kh = [AR1[:, 2048:4096], AR1[:, 4096:6144]]
        vh = [AR1[:, 6144:8192], AR1[:, 8192:10240]]
        vt = AR1[:, 10240:12288]
        Pt = AR1[:, 12288:14336]
        PT = AR1[:, 14336:22528].rearrange("p (c n) -> p c n", c=16)
        oh = AR1[:, 22528:24576]
        wdl = [AR2[:, 5504 * i:5504 * (i + 1)] for i in range(2)]
        Pt2 = AR2[:, 11008:13056]
        wdq = AR2[:, 13056:18560]
        Pt3 = AR2[:, 18560:20608]
        Pts = [Pt, Pt2, Pt3]

        pending = []
        for h in range(H):
            b = h % 2
            with T.dma_batch("sync"):
                T.dma("sync", qh[:], qs[h, :, :], reads=[f"qsc{h}"], writes=["qh"],
                      tag=f"ldq{h}")
                T.dma("sync", kh[b][:], ks[h, :, :], reads=[f"ksc{h}"], writes=[f"kh{b}"],
                      tag=f"ldk{h}")
                T.dma("sync", vh[b][:], vs[h, :, :], reads=[f"vsc{h}"], writes=[f"vh{b}"],
                      tag=f"ldv{h}")
            al = ["y"] if h < 2 else []
            T.dma("sync", wdl[b][:], wd_h[128 * h:128 * (h + 1), :],
                  writes=[f"wdl{b}"] + al, tag=f"ldwd{h}")
            for j in range(16):
                T.emit("tensor", lambda t, j=j, h=h: t.matmul(
                    psb(0)[:, 32 * j:32 * (j + 1)], qh[:, 128 * j:128 * (j + 1)],
                    KG[:, NB * h:NB * (h + 1)], start=(j == 0), stop=(j == 15)),
                    reads=["qh", "kg"], writes=["psB0"], tag=f"gate{h}")
            for j in range(16):
                T.emit("vector", lambda v, j=j: v.scalar_tensor_tensor(
                    GW[:, 32 * j:32 * (j + 1)], psb(0)[:, 32 * j:32 * (j + 1)], 1.0 / 64,
                    CO[:, 32 - 2 * j:64 - 2 * j], op0=ALU.mult, op1=ALU.add),
                    reads=["psB0", "co"], writes=["gw"])
                T.emit("vector", lambda v, j=j: v.max(
                    T8[:, 8 * j:8 * (j + 1)], GW[:, 32 * j:32 * (j + 1)]),
                    reads=["gw"], writes=["t8"])
                T.emit("vector", lambda v, j=j: v.tensor_scalar(
                    BB[:, 32 * j:32 * (j + 1)], GW[:, 32 * j:32 * (j + 1)],
                    T8[:, 8 * j + 7:8 * j + 8], -BIG, op0=ALU.is_lt, op1=ALU.mult),
                    reads=["gw", "t8"], writes=["bb"])
            for kc4 in range(4):
                for ff in range(4):
                    kc = kc4 * 4 + ff
                    T.emit("tensor", lambda t, b=b, kc=kc, ff=ff: t.transpose(
                        psb(1)[:, 128 * ff:128 * (ff + 1)],
                        vh[b][:, 128 * kc:128 * (kc + 1)], ident),
                        reads=[f"vh{b}", "co"], writes=["psB1"])
                T.emit("scalar", lambda s, kc4=kc4: s.activation(
                    vt[:, 512 * kc4:512 * (kc4 + 1)], psb(1)[:, 0:512], AF.Copy),
                    reads=["psB1"], writes=["vt"])
            slice_ctr = [0]

            def emit_A(j):
                kend = 128 * (j + 1)
                par = j % 3
                ptb = Pts[par]
                nsl = list(range(0, kend, 512))
                for n0 in nsl:
                    n1 = min(kend, n0 + 512)
                    bk = slice_ctr[0] % 4
                    slice_ctr[0] += 1
                    last = (n0 == nsl[-1])
                    psl = PSA[:, 512 * bk:512 * bk + (n1 - n0)]
                    T.emit("tensor", lambda t, b=b, j=j, n0=n0, n1=n1, psl=psl,
                           last=last: t.matmul(
                        psl, qh[:, 128 * j:128 * (j + 1)], kh[b][:, n0:n1],
                        start=True, stop=not last),
                        reads=["qh", f"kh{b}"], writes=[f"psA{bk}"], tag=f"sc{h}_{j}")
                    if last:
                        w = n1 - n0
                        T.emit("tensor", lambda t, psl=psl, w=w: t.matmul(
                            psl[:, w - 128:w], ident, causal,
                            start=False, stop=True),
                            reads=["co"], writes=[f"psA{bk}"], tag=f"cz{h}_{j}")
                    nb0 = n0 // 64
                    nbn = (n1 - n0) // 64
                    T.emit("vector", lambda v, j=j, n0=n0, n1=n1, psl=psl, par=par,
                           ptb=ptb, nb0=nb0, nbn=nbn: v.tensor_tensor(
                        ptb[:, n0:n1].rearrange("p (b w) -> p b w", w=64),
                        psl.rearrange("p (b w) -> p b w", w=64),
                        BB[:, 32 * j + nb0:32 * j + nb0 + nbn]
                        .rearrange("p (b o) -> p b o", o=1)
                        .broadcast_to([128, nbn, 64]),
                        op=ALU.add), reads=[f"psA{bk}", "bb"], writes=[f"Pt{par}"])
                T.emit("vector", lambda v, kend=kend, par=par, ptb=ptb: v.reduce_max(
                    SMALL[:, 20 + 3 * par:21 + 3 * par], ptb[:, 0:kend], axis=AX.X,
                    negate=True), reads=[f"Pt{par}"], writes=[f"a_m{par}"])
                T.emit("scalar", lambda s, kend=kend, par=par, ptb=ptb: s.activation(
                    ptb[:, 0:kend], ptb[:, 0:kend], AF.Exp,
                    bias=SMALL[:, 20 + 3 * par:21 + 3 * par], scale=1.0,
                    accum_out=SMALL[:, 21 + 3 * par:22 + 3 * par]),
                    reads=[f"Pt{par}", f"a_m{par}"],
                    writes=[f"Pt{par}", f"a_sig{par}"])

            def emit_B(jb):
                kend = 128 * (jb + 1)
                par = jb % 3
                jjb = jb % 4
                ptb = Pts[par]
                T.emit("vector", lambda v, par=par: v.reciprocal(
                    SMALL[:, 22 + 3 * par:23 + 3 * par],
                    SMALL[:, 21 + 3 * par:22 + 3 * par]),
                    reads=[f"a_sig{par}"], writes=[f"a_inv{par}"])
                T.emit("vector", lambda v, kend=kend, par=par, ptb=ptb: v.tensor_scalar(
                    ptb[:, 0:kend], ptb[:, 0:kend],
                    SMALL[:, 22 + 3 * par:23 + 3 * par], None, op0=ALU.mult),
                    reads=[f"Pt{par}", f"a_inv{par}"], writes=[f"Pt{par}"])
                for kc0 in range(0, jb + 1, 4):
                    kcn = min(4, jb + 1 - kc0)
                    bk = 2 + ((kc0 // 4) % 2)
                    for ff in range(kcn):
                        kc = kc0 + ff
                        T.emit("tensor", lambda t, kc=kc, ff=ff, ptb=ptb, bk=bk: t.transpose(
                            psb(bk)[:, 128 * ff:128 * (ff + 1)],
                            ptb[:, 128 * kc:128 * (kc + 1)], ident),
                            reads=[f"Pt{par}", "co"], writes=[f"psB{bk}"])
                    T.emit("scalar", lambda s, kc0=kc0, kcn=kcn, jjb=jjb, bk=bk: s.activation(
                        PT[:, kc0:kc0 + kcn, 128 * jjb:128 * (jjb + 1)],
                        psb(bk)[:, 0:128 * kcn].rearrange("p (c n) -> p c n", c=kcn),
                        AF.Copy),
                        reads=[f"psB{bk}"], writes=["PT"])

            def emit_PV(g):
                nkc_g = 4 * g + 4
                for kc in range(nkc_g):
                    col0 = 128 * max(0, kc - 4 * g)
                    T.emit("tensor", lambda t, kc=kc, col0=col0, nkc_g=nkc_g: t.matmul(
                        psb(3)[:, col0:512], vt[:, 128 * kc:128 * (kc + 1)],
                        PT[:, kc, col0:512],
                        start=(kc == 0), stop=(kc == nkc_g - 1)),
                        reads=["vt", "PT"], writes=["psB3"], tag=f"pv{h}_{g}")
                T.emit("scalar", lambda s, g=g: s.activation(
                    oh[:, 512 * g:512 * (g + 1)], psb(3)[:, 0:512], AF.Copy),
                    reads=["psB3"], writes=["oh"])
                T.dma("sync", os_[g, h, :, :], oh[:, 512 * g:512 * (g + 1)],
                      reads=["oh"], writes=[f"ossc{g}_{h}"], tag=f"sto{h}_{g}")

            for j in range(16):
                jb = j - 2
                if jb >= 0:
                    emit_B(jb)
                    if jb % 4 == 3:
                        emit_PV(jb // 4)
                if pending:
                    pending.pop(0)()
                emit_A(j)
                if pending:
                    pending.pop(0)()
            emit_B(14)
            emit_B(15)
            emit_PV(3)
            while pending:
                pending.pop(0)()
            dc = h
            ternary_quant(wdl[b], f"wdl{b}", wdq, "wdq", NFC, sink=pending)

            def wd_evac(fc0, dc=dc):
                fcn = min(4, NFC - fc0)
                for ff in range(fcn):
                    fc = fc0 + ff
                    T.emit("tensor", lambda t, fc=fc, ff=ff: t.transpose(
                        psb(1)[:, 128 * ff:128 * (ff + 1)],
                        wdq[:, 128 * fc:128 * (fc + 1)], ident),
                        reads=["wdq", "co"], writes=["psB1"])
                T.emit("scalar", lambda s, fcn=fcn: s.activation(
                    WDTBH[:, 0:128 * fcn], psb(1)[:, 0:128 * fcn], AF.Copy),
                    reads=["psB1"], writes=["wdtbh"])
                T.emit("vector", lambda v, fcn=fcn: v.tensor_sub(
                    WDTBL[:, 0:128 * fcn], psb(1)[:, 0:128 * fcn], WDTBH[:, 0:128 * fcn]),
                    reads=["psB1", "wdtbh"], writes=["wdtbl"])
                T.dma("sync", wds[dc, 0, fc0:fc0 + fcn, :, :].rearrange("c p n -> p c n"),
                      WDTBH[:, 0:128 * fcn].rearrange("p (c n) -> p c n", c=fcn),
                      reads=["wdtbh"], writes=[f"wdsc{dc}"], tag=f"stwdh{dc}_{fc0}")
                T.dma("sync", wds[dc, 1, fc0:fc0 + fcn, :, :].rearrange("c p n -> p c n"),
                      WDTBL[:, 0:128 * fcn].rearrange("p (c n) -> p c n", c=fcn),
                      reads=["wdtbl"], writes=[f"wdsc{dc}"], tag=f"stwdl{dc}_{fc0}")

            for fc0 in range(0, NFC, 4):
                pending.append(lambda fc0=fc0: wd_evac(fc0))
        while pending:
            pending.pop(0)()

        dbg_dump(T, ["os", "wds"])
        T.phase()

        osg = AR1[:, 0:8192].rearrange("p (c n) -> p c n", c=16)
        xtm = AR1[:, 8192:16384].rearrange("p (c n) -> p c n", c=4)
        xtmf = AR1[:, 8192:16384]
        xfm2 = [AR1[:, 16384:24576].rearrange("p (c n) -> p c n", c=16),
                AR2[:, 12288:20480].rearrange("p (c n) -> p c n", c=16)]
        xfm2f = [AR1[:, 16384:24576], AR2[:, 12288:20480]]
        x1g = AR2[:, 0:8192].rearrange("p (c n) -> p c n", c=16)
        woc = [AR2[:, 8192 + 2048 * i:8192 + 2048 * (i + 1)]
               .rearrange("p (c n) -> p c n", c=16) for i in range(2)]
        SR3 = AR2[:, 20480:22528]

        def p3_xload(g):
            par = g % 2
            xf = xfm2[par]
            T.dma("sync", xtm[:, :, :],
                  x_h[512 * g:512 * g + 512, :].rearrange("(c p) n -> p c n", p=128),
                  writes=["xtm"], tag=f"ldx2_{g}")
            for cc in range(4):
                for fq in range(4):
                    for ff in range(4):
                        fc = fq * 4 + ff
                        T.emit("tensor", lambda t, cc=cc, fc=fc, ff=ff: t.transpose(
                            psb(0)[:, 128 * ff:128 * (ff + 1)],
                            xtm[:, cc, 128 * fc:128 * (fc + 1)], ident),
                            reads=["xtm", "co"], writes=["psB0"])
                    T.emit("vector", lambda v, cc=cc, fq=fq, xf=xf: v.tensor_copy(
                        xf[:, 4 * fq:4 * fq + 4, 128 * cc:128 * (cc + 1)],
                        psb(0)[:, 0:512].rearrange("p (c n) -> p c n", c=4)),
                        reads=["psB0"],
                        writes=[f"xf{par}_{4 * fq + i}" for i in range(4)])

        def p3_store(gq):
            parq = gq % 2
            with T.dma_batch("gpsimd"):
                for dc in range(NDC):
                    T.dma("gpsimd", xqs[gq, dc, :, :], xfm2[parq][:, dc, :],
                          reads=[f"xf{parq}_{dc}"], writes=[f"xqsc{gq}"],
                          tag=f"stxq{gq}_{dc}")

        T.dma("sync", osg[:, :, :], os_[0, :, :, :].rearrange("h p n -> p h n"),
              reads=[f"ossc0_{hh}" for hh in range(H)], writes=["osg"], tag="ldo0")
        p3_xload(0)
        for g in range(4):
            par = g % 2
            parq = 1 - par
            xf = xfm2[par]
            for grp in range(4):
                if g > 0:
                    ladder(xfm2f[parq][:, 2048 * grp:2048 * (grp + 1)],
                           xtmf[:, 0:4096],
                           [f"xf{parq}_{4 * grp + i}" for i in range(4)],
                           2048, tmp_key="xtm")
                for dc in range(4 * grp, 4 * grp + 4):
                    wb = dc % 2
                    pa = dc % 2
                    T.dma("sync", woc[wb][:, :, :],
                          wo_h[:, 128 * dc:128 * (dc + 1)]
                          .rearrange("(c p) n -> p c n", p=128),
                          writes=[f"woc{wb}"], tag=f"ldwo{g}_{dc}")
                    for hd in range(H):
                        T.emit("tensor", lambda t, wb=wb, hd=hd, pa=pa: t.matmul(
                            psb(pa)[:, 0:512], woc[wb][:, hd, :], osg[:, hd, :],
                            start=(hd == 0), stop=(hd == H - 1)),
                            reads=[f"woc{wb}", "osg"], writes=[f"psB{pa}"],
                            tag=f"wo{g}_{dc}")
                    T.emit("vector", lambda v, dc=dc, pa=pa, xf=xf: v.tensor_add(
                        x1g[:, dc, :], psb(pa)[:, 0:512], xf[:, dc, :]),
                        reads=[f"psB{pa}", f"xf{par}_{dc}"], writes=[f"x1g_{dc}"])
                    T.emit("tensor", lambda t, dc=dc: t.matmul(
                        psb(2)[:, 0:512], ONES[:], x1g[:, dc, :],
                        start=(dc == 0), stop=(dc == NDC - 1)),
                        reads=["ones", f"x1g_{dc}"], writes=["psB2"], tag=f"lns{g}")
                    T.emit("scalar", lambda s, dc=dc, xf=xf: s.activation(
                        xf[:, dc, :], x1g[:, dc, :], AF.Square),
                        reads=[f"x1g_{dc}"], writes=[f"xf{par}_{dc}"])
                    T.emit("tensor", lambda t, dc=dc, xf=xf: t.matmul(
                        psb(3)[:, 0:512], ONES[:], xf[:, dc, :],
                        start=(dc == 0), stop=(dc == NDC - 1)),
                        reads=["ones", f"xf{par}_{dc}"], writes=["psB3"],
                        tag=f"lnsq{g}")
                    T.dma("sync", x1s[g, dc, :, :], x1g[:, dc, :],
                          reads=[f"x1g_{dc}"], writes=[f"x1sc{g}"],
                          tag=f"stx1_{g}_{dc}")
            if g > 0:
                p3_store(g - 1)
            if g < 3:
                T.dma("sync", osg[:, :, :],
                      os_[g + 1, :, :, :].rearrange("h p n -> p h n"),
                      reads=[f"ossc{g + 1}_{hh}" for hh in range(H)],
                      writes=["osg"], tag=f"ldo{g + 1}")
            T.emit("vector", lambda v: v.tensor_scalar(
                MREP[:], psb(2)[:, 0:512], 1.0 / DM, None, op0=ALU.mult),
                reads=["psB2"], writes=["mrep"])
            T.emit("vector", lambda v: v.tensor_mul(SR3[:, 0:512], MREP[:], MREP[:]),
                   reads=["mrep"], writes=["sr3a"])
            T.emit("vector", lambda v: v.scalar_tensor_tensor(
                RREP[:], psb(3)[:, 0:512], 1.0 / DM, SR3[:, 0:512],
                op0=ALU.mult, op1=ALU.subtract),
                reads=["psB3", "sr3a"], writes=["rrep"])
            if probe_ap is not None:
                T.dma("gpsimd", probe_ap[g, :, 1024:1536], RREP[:], reads=["rrep"], tag="pbv")
                T.dma("gpsimd", probe_ap[g, :, 1536:2048], SR3[:, 0:512], reads=["sr3a"], tag="pbm2")
            T.emit("scalar", lambda s: s.activation(RREP[:], RREP[:], AF.Sqrt, bias=EPSC),
                   reads=["rrep", "epsc"], writes=["rrep"])
            T.emit("vector", lambda v: v.reciprocal(SR3[:, 0:512], RREP[:]),
                   reads=["rrep"], writes=["sr3a"])
            T.emit("vector", lambda v: v.tensor_mul(SR3[:, 512:1024], RREP[:], SR3[:, 0:512]),
                   reads=["rrep", "sr3a"], writes=["p3u"])
            T.emit("vector", lambda v: v.tensor_mul(SR3[:, 1024:1536], SR3[:, 512:1024],
                                                    SR3[:, 512:1024]),
                   reads=["p3u"], writes=["p3t"])
            T.emit("vector", lambda v: v.tensor_scalar(
                SR3[:, 1024:1536], SR3[:, 1024:1536], -0.5, 1.5, op0=ALU.mult, op1=ALU.add),
                reads=["p3t"], writes=["p3t"])
            T.emit("vector", lambda v: v.tensor_mul(RREP[:], SR3[:, 0:512], SR3[:, 1024:1536]),
                   reads=["sr3a", "p3t"], writes=["rrep"])
            if probe_ap is not None:
                T.dma("gpsimd", probe_ap[g, :, 0:512], MREP[:], reads=["mrep"], tag="pb1")
                T.dma("gpsimd", probe_ap[g, :, 512:1024], RREP[:], reads=["rrep"], tag="pb2")
            T.emit("vector", lambda v: v.memset(SR3[:, 1536:2048], 0.0), writes=["p3m16"])
            T.emit("vector", lambda v: v.memset(SR3[:, 512:1024], 0.0), writes=["p3u"])
            for dc in range(NDC):
                T.emit("vector", lambda v, dc=dc, xf=xf: v.tensor_sub(
                    xf[:, dc, :], x1g[:, dc, :], MREP[:]),
                    reads=[f"x1g_{dc}", "mrep"], writes=[f"xf{par}_{dc}"])
                T.emit("vector", lambda v, dc=dc, xf=xf: v.tensor_mul(
                    xf[:, dc, :], xf[:, dc, :], RREP[:]),
                    reads=[f"xf{par}_{dc}", "rrep"], writes=[f"xf{par}_{dc}"])
                T.emit("vector", lambda v, dc=dc, xf=xf: v.tensor_max(
                    SR3[:, 1536:2048], SR3[:, 1536:2048], xf[:, dc, :]),
                    reads=[f"xf{par}_{dc}", "p3m16"], writes=["p3m16"])
                T.emit("vector", lambda v, dc=dc, xf=xf: v.tensor_tensor(
                    SR3[:, 512:1024], SR3[:, 512:1024], xf[:, dc, :], op=ALU.min),
                    reads=[f"xf{par}_{dc}", "p3u"], writes=["p3u"])
            T.emit("vector", lambda v: v.tensor_scalar(
                SR3[:, 512:1024], SR3[:, 512:1024], -1.0, None, op0=ALU.mult),
                reads=["p3u"], writes=["p3u"])
            T.emit("vector", lambda v: v.tensor_max(
                SR3[:, 1536:2048], SR3[:, 1536:2048], SR3[:, 512:1024]),
                reads=["p3u", "p3m16"], writes=["p3m16"])
            for ts in range(4):
                T.emit("tensor", lambda t, ts=ts: t.transpose(
                    psb(0)[:, 0:128], SR3[:, 1536 + 128 * ts:1536 + 128 * (ts + 1)], ident),
                    reads=["p3m16", "co"], writes=["psB0"])
                T.emit("vector", lambda v, ts=ts: v.reduce_max(
                    SM2[:, ts:ts + 1], psb(0)[:, 0:128], axis=AX.X),
                    reads=["psB0"], writes=["amax1"])
            T.emit("vector", lambda v: v.tensor_scalar(
                SM2[:, 0:4], SM2[:, 0:4], EPS, None, op0=ALU.max),
                reads=["amax1"], writes=["amax1"])
            T.emit("tensor", lambda t: t.transpose(psb(1)[:, 0:128], SM2[:, 0:128], ident),
                   reads=["amax1", "co"], writes=["psB1"])
            T.emit("vector", lambda v: v.tensor_copy(SM2[:, 128:256], psb(1)[:, 0:128]),
                   reads=["psB1"], writes=["amaxT"])
            with T.dma_batch("gpsimd"):
                for c in range(4):
                    T.dma("gpsimd", SR3[0:1, 128 * c:128 * (c + 1)], SM2[c:c + 1, 128:256],
                          reads=["amaxT"], writes=["sr3a"], tag=f"af{g}_{c}")
            T.emit("tensor", lambda t: t.matmul(
                psb(1)[:, 0:512], ONES[0:1, :], SR3[0:1, 0:512], start=True, stop=True),
                reads=["sr3a", "ones"], writes=["psB1"], tag=f"rep{g}")
            T.emit("vector", lambda v: v.tensor_scalar(
                MREP[:], psb(1)[:, 0:512], 1.0 / 127.0, None, op0=ALU.mult),
                reads=["psB1"], writes=["mrep"])
            T.emit("vector", lambda v: v.reciprocal(RREP[:], MREP[:]),
                   reads=["mrep"], writes=["rrep"])
            T.dma("sync", ss1[g, :, :], MREP[:], reads=["mrep"], writes=[f"ss1c{g}"],
                  tag=f"stss1_{g}")
            for dc in range(NDC):
                T.emit("vector", lambda v, dc=dc, xf=xf: v.tensor_mul(
                    xf[:, dc, :], xf[:, dc, :], RREP[:]),
                    reads=[f"xf{par}_{dc}", "rrep"], writes=[f"xf{par}_{dc}"])
            if g < 3:
                p3_xload(g + 1)
        for grp in range(4):
            ladder(xfm2f[1][:, 2048 * grp:2048 * (grp + 1)], xtmf[:, 0:4096],
                   [f"xf1_{4 * grp + i}" for i in range(4)], 2048, tmp_key="xtm")
        p3_store(3)

        dbg_dump(T, ["x1s", "xqs"])
        T.phase()

        AR2_ctx.__exit__(None, None, None)
        BFB_ctx = nc.sbuf_tensor("bfarena", [128, 44032], BF16)
        BFB = BFB_ctx.__enter__()
        wl4 = [AR1[:, 2048 * i:2048 * (i + 1)] for i in range(2)]
        tq4 = [AR1[:, 4096 + 2048 * i:4096 + 2048 * (i + 1)] for i in range(2)]
        wqT32g = AR1[:, 8192:10240]
        wqT32u = AR1[:, 10240:12288]
        gs_b = AR1[:, 12288:14336]
        hb = AR1[:, 14336:16384]
        M43 = AR1[:, 16384:18432]
        is1 = AR1[:, 18432:20480]
        M43N = AR1[:, 20480:22528]
        xqb = BFB[:, 0:32768].rearrange("p (c n) -> p c n", c=NDC)
        wqbg = [BFB[:, 32768 + 2048 * i:32768 + 2048 * (i + 1)] for i in range(2)]
        wqbu = [BFB[:, 36864 + 2048 * i:36864 + 2048 * (i + 1)] for i in range(2)]

        with T.dma_batch("sync"):
            for g in range(4):
                for dc in range(NDC):
                    T.dma("sync", xqb[:, dc, 512 * g:512 * (g + 1)], xqs[g, dc, :, :],
                          reads=[f"xqsc{g}"], writes=["xqb"], tag=f"ldxq{g}_{dc}")
        T.emit("vector", lambda v: v.memset(M43[:], 0.0),
               writes=["m43h0", "m43h1"])
        T.emit("vector", lambda v: v.memset(M43N[:], 0.0),
               writes=["m43n0", "m43n1"])
        with T.dma_batch("sync"):
            for g in range(4):
                T.dma("sync", is1[:, 512 * g:512 * (g + 1)], ss1[g, :, :],
                      reads=[f"ss1c{g}"], writes=["is1"], tag=f"ldss1_{g}")

        pend4 = []

        def gateup_quant(w_hh, fc, par, dbkey):
            T.dma("sync", wl4[par][:], w_hh[128 * fc:128 * (fc + 1), :],
                  writes=[f"wl4{par}"], tag=f"ld{dbkey}{fc}")
            ternary_quant(wl4[par], f"wl4{par}", tq4[par], f"tq4{par}", NDC,
                          sink=pend4)

        def gateup_finish(par, dst32, d32key, dstb, dbkey, banks):
            for i4, c0 in enumerate(range(0, 16, 4)):
                bk = banks[i4 % 2]
                for ff in range(4):
                    c = c0 + ff
                    T.emit("tensor", lambda t, c=c, ff=ff, par=par, bk=bk: t.transpose(
                        psb(bk)[:, 128 * ff:128 * (ff + 1)],
                        tq4[par][:, 128 * c:128 * (c + 1)], ident),
                        reads=[f"tq4{par}", "co"], writes=[f"psB{bk}"])
                T.emit("scalar", lambda s, c0=c0, dst32=dst32, bk=bk: s.activation(
                    dst32[:, 128 * c0:128 * c0 + 512], psb(bk)[:, 0:512], AF.Copy),
                    reads=[f"psB{bk}"], writes=[d32key])
            T.emit("vector", lambda v, dst32=dst32, dstb=dstb: v.tensor_copy(
                dstb[0][:], dst32[:]), reads=[d32key], writes=[dbkey + "h"])
            T.emit("vector", lambda v, dst32=dst32, dstb=dstb: v.tensor_sub(
                dstb[1][:], dst32[:], dstb[0][:]),
                reads=[d32key, dbkey + "h"], writes=[dbkey + "l"])

        def pop4(n=2):
            for _ in range(n):
                if pend4:
                    pend4.pop(0)()

        gateup_quant(wg_h, 0, 0, "wqbg")
        gateup_quant(wu_h, 0, 1, "wqbu")
        while pend4:
            pend4.pop(0)()
        gateup_finish(0, wqT32g, "wqT32g", wqbg, "wqbg", (0, 1))
        gateup_finish(1, wqT32u, "wqT32u", wqbu, "wqbu", (2, 3))
        for fc in range(NFC):
            if fc + 1 < NFC:
                gateup_quant(wg_h, fc + 1, 0, "wqbg")
                gateup_quant(wu_h, fc + 1, 1, "wqbu")
            for nh in range(2):
                for sl in range(2):
                    ns = 2 * nh + sl
                    for din in range(NDC):
                        for hl in range(2):
                            T.emit("tensor", lambda t, sl=sl, ns=ns, din=din, hl=hl: t.matmul(
                                PSA[:, 512 * sl:512 * (sl + 1)],
                                wqbg[hl][:, 128 * din:128 * (din + 1)],
                                xqb[:, din, 512 * ns:512 * (ns + 1)],
                                start=(din == 0 and hl == 0),
                                stop=(din == NDC - 1 and hl == 1)),
                                reads=["wqbgh", "wqbgl", "xqb"], writes=["psAg"],
                                tag=f"gmm{fc}")
                for sl in range(2):
                    ns = 2 * nh + sl
                    for din in range(NDC):
                        for hl in range(2):
                            T.emit("tensor", lambda t, sl=sl, ns=ns, din=din, hl=hl: t.matmul(
                                PSA[:, 1024 + 512 * sl:1024 + 512 * (sl + 1)],
                                wqbu[hl][:, 128 * din:128 * (din + 1)],
                                xqb[:, din, 512 * ns:512 * (ns + 1)],
                                start=(din == 0 and hl == 0),
                                stop=(din == NDC - 1 and hl == 1)),
                                reads=["wqbuh", "wqbul", "xqb"], writes=["psAu"],
                                tag=f"umm{fc}")
                h0 = 1024 * nh
                T.emit("vector", lambda v, h0=h0: v.tensor_mul(
                    gs_b[:, h0:h0 + 1024], PSA[:, 0:1024], is1[:, h0:h0 + 1024]),
                    reads=["psAg", "is1"], writes=[f"gs{nh}"])
                pop4()
                T.emit("scalar", lambda s, h0=h0: s.activation(
                    gs_b[:, h0:h0 + 1024], gs_b[:, h0:h0 + 1024], AF.Silu),
                    reads=[f"gs{nh}"], writes=[f"gs{nh}"])
                T.emit("vector", lambda v, h0=h0: v.tensor_mul(
                    hb[:, h0:h0 + 1024], PSA[:, 1024:2048], is1[:, h0:h0 + 1024]),
                    reads=["psAu", "is1"], writes=[f"hb{nh}"])
                pop4()
                T.emit("vector", lambda v, h0=h0: v.tensor_mul(
                    hb[:, h0:h0 + 1024], gs_b[:, h0:h0 + 1024], hb[:, h0:h0 + 1024]),
                    reads=[f"gs{nh}", f"hb{nh}"], writes=[f"hb{nh}"])
                pop4()
                T.emit("vector", lambda v, h0=h0: v.tensor_max(
                    M43[:, h0:h0 + 1024], M43[:, h0:h0 + 1024], hb[:, h0:h0 + 1024]),
                    reads=[f"hb{nh}", f"m43h{nh}"], writes=[f"m43h{nh}"])
                pop4()
                T.emit("vector", lambda v, h0=h0: v.tensor_tensor(
                    M43N[:, h0:h0 + 1024], M43N[:, h0:h0 + 1024],
                    hb[:, h0:h0 + 1024], op=ALU.min),
                    reads=[f"hb{nh}", f"m43n{nh}"], writes=[f"m43n{nh}"])
                pop4()
            while pend4:
                pend4.pop(0)()
            if fc + 1 < NFC:
                gateup_finish(0, wqT32g, "wqT32g", wqbg, "wqbg", (0, 1))
                gateup_finish(1, wqT32u, "wqT32u", wqbu, "wqbu", (2, 3))
            T.dma("sync", hs[fc, :, :], hb[:], reads=["hb0", "hb1"],
                  writes=[f"hsc{fc}"], tag=f"sth{fc}")

        dbg_dump(T, ["hs"])

        T.emit("vector", lambda v: v.tensor_scalar(
            M43N[:], M43N[:], -1.0, None, op0=ALU.mult),
            reads=["m43n0", "m43n1"], writes=["m43n0", "m43n1"])
        T.emit("vector", lambda v: v.tensor_max(M43[:], M43[:], M43N[:]),
               reads=["m43n0", "m43n1", "m43h0", "m43h1"],
               writes=["m43h0", "m43h1"])
        for ts in range(16):
            T.emit("tensor", lambda t, ts=ts: t.transpose(
                psb(0)[:, 0:128], M43[:, 128 * ts:128 * (ts + 1)], ident),
                reads=["m43h0", "m43h1", "co"], writes=["psB0"])
            T.emit("vector", lambda v, ts=ts: v.reduce_max(
                SM2[:, ts:ts + 1], psb(0)[:, 0:128], axis=AX.X),
                reads=["psB0"], writes=["amax2", "q_s0", "q_sc"])
        T.emit("vector", lambda v: v.tensor_scalar(
            SM2[:, 0:16], SM2[:, 0:16], EPS, None, op0=ALU.max),
            reads=["amax2"], writes=["amax2"])
        T.emit("tensor", lambda t: t.transpose(psb(1)[:, 0:128], SM2[:, 0:128], ident),
               reads=["amax2", "co"], writes=["psB1"])
        T.emit("vector", lambda v: v.tensor_copy(SM2[:, 128:256], psb(1)[:, 0:128]),
               reads=["psB1"], writes=["amax2T", "q_thr", "q_nthr"])
        T.phase()

        sstg4 = [AR1[:, 2048 * i:2048 * (i + 1)] for i in range(2)]
        ltmp = AR1[:, 4096:8192]
        outtm = AR1[:, 8192:16384]
        outtm3 = outtm.rearrange("p (c n) -> p c n", c=4)
        s2rep = AR1[:, 16384:18432]
        is2rep = AR1[:, 18432:20480]
        hld = [AR1[:, 20480 + 512 * i:20480 + 512 * (i + 1)] for i in range(2)]
        x1l = [AR1[:, 21504 + 512 * i:21504 + 512 * (i + 1)] for i in range(2)]
        otmp2 = [AR1[:, 22528 + 512 * i:22528 + 512 * (i + 1)] for i in range(2)]
        hqb3 = BFB[:, 0:22016].rearrange("p (c n) -> p c n", c=NFC)
        wq4 = [[BFB[:, 22016 + 8192 * s2_ + 1024 * l_:
                    22016 + 8192 * s2_ + 1024 * (l_ + 1)]
                for l_ in range(8)] for s2_ in range(2)]

        def lanebank(l):
            return PSA[:, 512 * l:512 * (l + 1)] if l < 4 else \
                PSB[:, 512 * (l - 4):512 * (l - 3)]

        def lkey(l):
            return f"psA{l}" if l < 4 else f"psB{l - 4}"

        with T.dma_batch("gpsimd"):
            for c in range(16):
                T.dma("gpsimd", s2rep[0:1, 128 * c:128 * (c + 1)], SM2[c:c + 1, 128:256],
                      reads=["amax2T"], writes=["amax2F"], tag=f"af2_{c}")
        for ns in range(4):
            T.emit("tensor", lambda t, ns=ns: t.matmul(
                PSA[:, 512 * ns:512 * (ns + 1)], ONES[0:1, :],
                s2rep[0:1, 512 * ns:512 * (ns + 1)], start=True, stop=True),
                reads=["amax2F", "ones"], writes=[f"psA{ns}"], tag=f"rep2_{ns}")
        T.emit("vector", lambda v: v.tensor_scalar(
            is2rep[:], PSA[:], 1.0 / 127.0, None, op0=ALU.mult),
            reads=["psA0", "psA1", "psA2", "psA3"], writes=["is2"])
        T.emit("vector", lambda v: v.reciprocal(s2rep[:], is2rep[:]),
               reads=["is2"], writes=["s2"])

        NCH = (NFC + 3) // 4

        def chdim(ch):
            fc0 = 4 * ch
            return fc0, min(4, NFC - fc0)

        def emit_ladder(g, ch):
            fc0, fcn = chdim(ch)
            par = ch % 2
            sstg = sstg4[par]
            for i in range(fcn):
                fc = fc0 + i
                hb2 = i % 2
                T.dma("sync", hld[hb2][:], hs[fc, :, 512 * g:512 * (g + 1)],
                      reads=[f"hsc{fc}"], writes=[f"hld{hb2}"], tag=f"ldh{g}_{fc}")
                T.emit("vector", lambda v, i=i, hb2=hb2, g=g, sstg=sstg: v.tensor_mul(
                    sstg[:, 512 * i:512 * (i + 1)], hld[hb2][:],
                    s2rep[:, 512 * g:512 * (g + 1)]),
                    reads=[f"hld{hb2}", "s2"], writes=[f"sstg{par}"])
            width = 512 * fcn
            ladder(sstg[:, 0:width], ltmp[:, 0:2 * width], f"sstg{par}", width,
                   tmp_key="ltmp", out=BFB[:, 512 * fc0:512 * fc0 + width],
                   out_key=f"hqc{ch}")

        def emit_half(g, ch, dc0):
            fc0, fcn = chdim(ch)
            par = ch % 2
            with T.dma_batch("sync"):
                for l in range(8):
                    dc = dc0 + l
                    for hl in range(2):
                        T.dma("sync", wq4[par][l][:, fcn * 128 * hl:fcn * 128 * (hl + 1)]
                              .rearrange("p (c n) -> p c n", c=fcn),
                              wds[dc, hl, fc0:fc0 + fcn, :, :]
                              .rearrange("c p n -> p c n"),
                              reads=[f"wdsc{dc}"], writes=[f"wq{par}_{l}"],
                              tag=f"ldw_{g}_{ch}_{dc}_{hl}")
            for l in range(8):
                for i in range(fcn):
                    for hl in range(2):
                        off = 128 * (fcn * hl + i)
                        T.emit("tensor", lambda t, l=l, par=par, off=off, i=i,
                               hl=hl, fc0=fc0, ch=ch, fcn=fcn: t.matmul(
                            lanebank(l), wq4[par][l][:, off:off + 128],
                            hqb3[:, fc0 + i, :],
                            start=(ch == 0 and i == 0 and hl == 0),
                            stop=(ch == NCH - 1 and i == fcn - 1 and hl == 1)),
                            reads=[f"wq{par}_{l}", f"hqc{ch}"], writes=[lkey(l)],
                            tag=f"dm_{g}_{dc0 + l}")

        def evac_lane(g, dc, l):
            wb = l % 2
            T.dma("sync", x1l[wb][:], x1s[g, dc, :, :], reads=[f"x1sc{g}"],
                  writes=[f"x1l{wb}"], tag=f"ldx1{g}_{dc}")
            T.emit("vector", lambda v, l=l, g=g, wb=wb: v.tensor_mul(
                otmp2[wb][:], lanebank(l)[:, 0:512],
                is2rep[:, 512 * g:512 * (g + 1)]),
                reads=[lkey(l), "is2"], writes=[f"otmp{wb}"])
            T.emit("vector", lambda v, wb=wb: v.tensor_add(
                otmp2[wb][:], otmp2[wb][:], x1l[wb][:]),
                reads=[f"otmp{wb}", f"x1l{wb}"], writes=[f"otmp{wb}"])
            for ff in range(4):
                T.emit("tensor", lambda t, l=l, wb=wb, ff=ff: t.transpose(
                    lanebank(l)[:, 128 * ff:128 * (ff + 1)],
                    otmp2[wb][:, 128 * ff:128 * (ff + 1)], ident),
                    reads=[f"otmp{wb}", "co"], writes=[lkey(l)])
            T.emit("scalar", lambda s, l=l, dc=dc: s.activation(
                outtm3[:, 0:4, 128 * dc:128 * (dc + 1)],
                lanebank(l)[:, 0:512].rearrange("p (c n) -> p c n", c=4), AF.Copy),
                reads=[lkey(l)], writes=["outtm"])

        def store_out(g):
            tok0 = 512 * g
            T.dma("sync", out_h[tok0:tok0 + 512, :].rearrange("(c p) n -> p c n", p=128),
                  outtm3[:, :, :], reads=["outtm"], writes=["out_h"], tag=f"stout{g}")

        for ch in range(5):
            emit_ladder(0, ch)
        for g in range(4):
            if g > 0:
                for l in range(8):
                    evac_lane(g - 1, 8 + l, l)
                store_out(g - 1)
            for ch in range(NCH):
                if ch >= 5:
                    emit_ladder(g, ch)
                emit_half(g, ch, 0)
            for l in range(8):
                evac_lane(g, l, l)
            for ch in range(NCH):
                emit_half(g, ch, 8)
                if g < 3 and ch <= 4:
                    emit_ladder(g + 1, ch)
        for l in range(8):
            evac_lane(3, 8 + l, l)
        store_out(3)

        T.barrier("sync")
        T.build()
        BFB_ctx.__exit__(None, None, None)
    return nc


def kernel(L0_attn_hidden_states, L0_attn_Wq, L0_attn_Wk, L0_attn_Wv, L0_attn_Wo,
           L0_ffn_w_gate, L0_ffn_w_up, L0_ffn_w_down, _dbg=(), _cores=8):
    x = np.ascontiguousarray(np.asarray(L0_attn_hidden_states, dtype=np.float32))
    consts = host_consts()
    wq = np.ascontiguousarray(np.asarray(L0_attn_Wq, np.float32))
    wk = np.ascontiguousarray(np.asarray(L0_attn_Wk, np.float32))
    wv = np.ascontiguousarray(np.asarray(L0_attn_Wv, np.float32))
    wo = np.ascontiguousarray(np.asarray(L0_attn_Wo, np.float32))
    wg = np.ascontiguousarray(np.asarray(L0_ffn_w_gate, np.float32))
    wu = np.ascontiguousarray(np.asarray(L0_ffn_w_up, np.float32))
    wd = np.ascontiguousarray(np.asarray(L0_ffn_w_down, np.float32))
    nc = build_program(dbg=_dbg)
    in_maps = [{"x": np.ascontiguousarray(x[b]), "wq": wq, "wk": wk, "wv": wv,
                "wo": wo, "wg": wg, "wu": wu, "wd": wd, "consts": consts}
               for b in range(_cores)]
    res = run_bass_kernel_spmd(nc, in_maps, core_ids=list(range(_cores)))
    out = np.stack([res.results[b]["out"] for b in range(_cores)], axis=0)
    if _dbg:
        return out, res
    return out


# version 16 (final)
# speedup vs baseline: 1.0322x; 1.0031x over previous
import numpy as np
from contextlib import ExitStack, contextmanager

import concourse.bass as bass
import concourse.mybir as mybir
from concourse.bass_utils import run_bass_kernel_spmd

F32 = mybir.dt.float32
BF16 = mybir.dt.bfloat16
AF = mybir.ActivationFunctionType
ALU = mybir.AluOpType
AX = mybir.AxisListType

H = 16
D = 128
S = 2048
DM = 2048
DFF = 5504
NDC = DM // 128
NFC = DFF // 128
NB = 32
SM = 128 ** -0.5
BIG = 1.0e30
EPS = 1e-5

ENGINES = ("tensor", "vector", "scalar", "gpsimd", "sync")
COMPUTE_SELFSYNC = ("vector", "scalar", "gpsimd")


class Tracker:
    def __init__(self, nc, dma_depth=4):
        self.nc = nc
        self.ops = {e: [] for e in ENGINES}
        self.cnt = {e: 0 for e in ENGINES}
        self.dma_depth = dma_depth
        self.dq = {q: {"batch": 0, "targets": [0] * dma_depth}
                   for q in ("sync", "scalar", "gpsimd")}
        self.last_write = {}
        self.readers = {}
        self.watermark = {e: {} for e in ENGINES}
        self._batch_ctx = None
        self.n_waits = 0

    def _label_val(self, label):
        if label.startswith("E:"):
            return self.cnt[label[2:]]
        _, q, s = label.split(":")
        return self.dq[q]["targets"][int(s)]

    def _edges(self, eng, reads, writes, self_label):
        waits = {}

        def need(label, val):
            if label == self_label and not (eng in COMPUTE_SELFSYNC and label.startswith("E:")):
                return
            if label == "E:tensor" and eng == "tensor":
                return
            if self.watermark[eng].get(label, 0) >= val:
                return
            if waits.get(label, 0) < val:
                waits[label] = val

        for k in reads:
            w = self.last_write.get(k)
            if w:
                need(*w)
        for k in writes:
            w = self.last_write.get(k)
            if w:
                need(*w)
            for label, val in self.readers.get(k, {}).items():
                need(label, val)
        for label, val in waits.items():
            self.watermark[eng][label] = val
        self.n_waits += len(waits)
        return waits

    def _record(self, label, reads, writes):
        v = self._label_val(label)
        for k in writes:
            self.last_write[k] = (label, v)
            self.readers[k] = {}
        for k in reads:
            self.readers.setdefault(k, {})[label] = v

    def emit(self, eng, fn, reads=(), writes=(), tag=None):
        self_label = f"E:{eng}"
        waits = self._edges(eng, reads, writes, self_label)
        self.cnt[eng] += 1
        self.ops[eng].append((waits, fn, self_label, 1, tag))
        self._record(self_label, reads, writes)
        return self.cnt[eng]

    @contextmanager
    def dma_batch(self, queue):
        assert self._batch_ctx is None
        q = self.dq[queue]
        s = q["batch"] % self.dma_depth
        q["batch"] += 1
        gate = q["targets"][s]
        self._batch_ctx = (queue, s, [], [], gate)
        try:
            yield
        finally:
            queue_, s_, rd, wr, _ = self._batch_ctx
            self._batch_ctx = None
            self._record(f"D:{queue_}:{s_}", rd, wr)

    def dma(self, queue, out, in_, reads=(), writes=(), tag=None, **kw):
        standalone = self._batch_ctx is None
        if standalone:
            ctx = self.dma_batch(queue)
            ctx.__enter__()
        queue_, s, rd, wr, gate = self._batch_ctx
        assert queue_ == queue
        label = f"D:{queue}:{s}"
        waits = self._edges(queue, reads, writes, label)
        if gate > 0 and self.watermark[queue].get(label, 0) < gate:
            waits[label] = max(waits.get(label, 0), gate)
            self.watermark[queue][label] = gate
        self.dq[queue]["targets"][s] += 16
        self.ops[queue].append(
            (waits, lambda h, o=out, i=in_, kw=kw: h.dma_start(out=o, in_=i, **kw),
             label, 16, tag))
        rd.extend(reads)
        wr.extend(writes)
        if standalone:
            ctx.__exit__(None, None, None)

    def barrier(self, eng, labels=None):
        if labels is None:
            labels = {}
            for e in ENGINES:
                if e != eng and self.cnt[e]:
                    labels[f"E:{e}"] = self.cnt[e]
            for qname, q in self.dq.items():
                for s, t in enumerate(q["targets"]):
                    if t:
                        labels[f"D:{qname}:{s}"] = t
        waits = {}
        for label, val in labels.items():
            if self.watermark[eng].get(label, 0) < val:
                waits[label] = val
                self.watermark[eng][label] = val
        if waits:
            self.ops[eng].append((waits, None, None, 0, "barrier"))

    def phase(self):
        for e in ENGINES:
            self.barrier(e)

    def build(self):
        nc = self.nc
        labels = set()
        for e in ENGINES:
            for waits, _, inc_label, _, _ in self.ops[e]:
                labels.update(waits)
                if inc_label:
                    labels.add(inc_label)
        with ExitStack() as stack:
            sems = {lb: stack.enter_context(nc.semaphore(lb.replace(":", "_")))
                    for lb in sorted(labels)}
            block = stack.enter_context(nc.Block())
            for eng in ENGINES:
                if not self.ops[eng]:
                    continue

                def make(e, ops_e):
                    def run(handle):
                        for waits, fn, inc_label, inc_amt, tag in ops_e:
                            for lb, v in sorted(waits.items()):
                                handle.wait_ge(sems[lb], v)
                            if fn is not None:
                                ins = fn(handle)
                                if inc_label is not None:
                                    ins.then_inc(sems[inc_label], inc_amt)
                    return run

                getattr(block, eng)(make(eng, self.ops[eng]))
        return sems


def host_consts():
    ext = np.zeros((128, 64), np.float32)
    for p in range(128):
        t_self = 32 + (p // 64)
        ext[p, t_self] = BIG
        ext[p, t_self + 1:] = -BIG
    causal = np.zeros((128, 128), np.float32)
    for q in range(128):
        causal[q, q + 1:] = -BIG
    ident = np.eye(128, dtype=np.float32)
    return np.concatenate([ext, causal, ident], axis=1)


def build_program(dbg=()):
    nc = bass.Bass("TRN2", target_bir_lowering=False)
    x_h = nc.dram_tensor("x", [S, DM], F32, kind="ExternalInput").ap()
    wq_h = nc.dram_tensor("wq", [DM, DM], F32, kind="ExternalInput").ap()
    wk_h = nc.dram_tensor("wk", [DM, DM], F32, kind="ExternalInput").ap()
    wv_h = nc.dram_tensor("wv", [DM, DM], F32, kind="ExternalInput").ap()
    wo_h = nc.dram_tensor("wo", [DM, DM], F32, kind="ExternalInput").ap()
    wg_h = nc.dram_tensor("wg", [DFF, DM], F32, kind="ExternalInput").ap()
    wu_h = nc.dram_tensor("wu", [DFF, DM], F32, kind="ExternalInput").ap()
    wd_h = nc.dram_tensor("wd", [DM, DFF], F32, kind="ExternalInput").ap()
    co_h = nc.dram_tensor("consts", [128, 320], F32, kind="ExternalInput").ap()
    out_h = nc.dram_tensor("out", [S, DM], F32, kind="ExternalOutput").ap()

    qs = nc.dram_tensor("qs", [H, 128, S], F32, kind="Internal").ap()
    ks = nc.dram_tensor("ks", [H, 128, S], F32, kind="Internal").ap()
    vs = nc.dram_tensor("vs", [H, 128, S], F32, kind="Internal").ap()
    os_ = nc.dram_tensor("os", [4, H, 128, 512], F32, kind="Internal").ap()
    x1s = nc.dram_tensor("x1s", [4, NDC, 128, 512], F32, kind="Internal").ap()
    xqs = nc.dram_tensor("xqs", [4, NDC, 128, 512], BF16, kind="Internal").ap()
    hs = nc.dram_tensor("hs", [NFC, 128, S], F32, kind="Internal").ap()
    wds = nc.dram_tensor("wds", [NDC, 2, NFC, 128, 128], BF16, kind="Internal").ap()
    ss1 = nc.dram_tensor("ss1", [4, 128, 512], F32, kind="Internal").ap()

    dbg_shapes = {"qs": [H, 128, S], "ks": [H, 128, S], "vs": [H, 128, S],
                  "os": [4, H, 128, 512], "x1s": [4, NDC, 128, 512],
                  "xqs": [4, NDC, 128, 512], "hs": [NFC, 128, S],
                  "probe": [4, 128, 2048]}
    dbg_src = {"qs": qs, "ks": ks, "vs": vs, "os": os_, "x1s": x1s,
               "xqs": xqs, "hs": hs, "wds": wds}
    dbg_aps = {n: nc.dram_tensor("dbg_" + n, dbg_shapes[n], F32,
                                 kind="ExternalOutput").ap() for n in dbg}
    probe_ap = dbg_aps.get("probe")

    def dbg_dump(T, names):
        for n in names:
            if n in dbg_aps:
                T.barrier("gpsimd")
                T.dma("gpsimd", dbg_aps[n][:], dbg_src[n][:], tag="dbg_" + n)

    AW = 49152
    with ExitStack() as st:
        e = st.enter_context
        AR1 = e(nc.sbuf_tensor("arena1", [128, 24576], F32))
        WDTBH = e(nc.sbuf_tensor("wdtbh", [128, 2048], BF16))
        WDTBL = e(nc.sbuf_tensor("wdtbl", [128, 2048], BF16))
        CO = e(nc.sbuf_tensor("co", [128, 320], F32))
        KG = e(nc.sbuf_tensor("kg", [128, H * NB], F32))
        GW = e(nc.sbuf_tensor("gw", [128, 512], F32))
        T8 = e(nc.sbuf_tensor("t8", [128, 128], F32))
        BB = e(nc.sbuf_tensor("bb", [128, 512], F32))
        SMALL = e(nc.sbuf_tensor("small", [128, 160], F32))
        SM2 = e(nc.sbuf_tensor("small2", [128, 256], F32))
        ONES = e(nc.sbuf_tensor("ones", [128, 128], F32))
        MREP = e(nc.sbuf_tensor("mrep", [128, 512], F32))
        RREP = e(nc.sbuf_tensor("rrep", [128, 512], F32))
        PSA = e(nc.psum_tensor("psA", [128, 2048], F32))
        PSB = e(nc.psum_tensor("psB", [128, 2048], F32))

        ident = CO[:, 192:320]
        causal = CO[:, 64:192]

        def psb(i):
            return PSB[:, 512 * i:512 * (i + 1)]

        T = Tracker(nc)
        T.dma("sync", CO[:], co_h[:, :], writes=["co"])
        T.emit("vector", lambda v: v.memset(ONES[:], 1.0), writes=["ones"])
        T.emit("vector", lambda v: v.memset(SM2[:], 0.0),
               writes=["amax1", "amax2", "q_s0", "q_sc", "q_thr", "q_nthr",
                       "amaxT", "amax2T"])
        T.emit("vector", lambda v: v.memset(SMALL[:], 0.0),
               writes=["epsc", "s_sum", "s_sq", "s_m", "s_m2", "s_v", "s_sq2", "s_r0",
                       "s_rr", "s_hv", "s_t", "s_r1", "a_m0", "a_m1", "a_m2",
                       "a_sig0", "a_sig1", "a_sig2", "a_inv0", "a_inv1", "a_inv2"])
        EPSC = SMALL[:, 159:160]
        T.emit("vector", lambda v: v.memset(EPSC, EPS), writes=["epsc"])

        def ladder(buf, tmp, key_buf, width, tmp_key, out=None, out_key=None):
            z0 = tmp[:, 0:width]
            tt = tmp[:, width:2 * width]
            kb = [key_buf] if isinstance(key_buf, str) else list(key_buf)
            tk = tmp_key
            T.emit("vector", lambda v: v.tensor_scalar(buf, buf, 128.5, None, op0=ALU.add),
                   reads=kb, writes=kb)
            T.emit("vector", lambda v: v.tensor_copy(z0, buf),
                   reads=kb, writes=[tk])
            for c in (256.0, 128.0, 64.0, 32.0, 16.0, 8.0, 4.0, 2.0, 1.0):
                T.emit("vector", lambda v, c=c: v.tensor_scalar(tt, buf, c, c,
                                                                op0=ALU.is_ge, op1=ALU.mult),
                       reads=kb, writes=[tk])
                T.emit("vector", lambda v: v.tensor_sub(buf, buf, tt),
                       reads=kb + [tk], writes=kb)
            T.emit("vector", lambda v: v.tensor_sub(buf, z0, buf),
                   reads=kb + [tk], writes=kb)
            dst = buf if out is None else out
            dk = kb if out_key is None else [out_key]
            T.emit("vector", lambda v: v.tensor_scalar(dst, buf, 128.0, None,
                                                       op0=ALU.subtract),
                   reads=kb, writes=dk)

        def ternary_quant(wbuf, key_w, tq_buf, key_tq, ngrp, sink=None):
            if sink is None:
                emit = T.emit
            else:
                emit = lambda *a, **kw: sink.append(lambda a=a, kw=kw: T.emit(*a, **kw))
            w3 = wbuf.rearrange("p (g w) -> p g w", w=128)
            emit("vector", lambda v: v.reduce_sum(SM2[:, 0:ngrp], w3, axis=AX.X,
                                                  apply_absolute_value=True),
                 reads=[key_w], writes=["q_s0"])
            emit("vector", lambda v: v.tensor_scalar(SM2[:, 64:64 + ngrp], SM2[:, 0:ngrp],
                                                     1.0 / 128, EPS,
                                                     op0=ALU.mult, op1=ALU.add),
                 reads=["q_s0"], writes=["q_sc"])
            emit("vector", lambda v: v.tensor_scalar(SM2[:, 128:128 + ngrp],
                                                     SM2[:, 64:64 + ngrp], 0.5, None,
                                                     op0=ALU.mult),
                 reads=["q_sc"], writes=["q_thr"])
            emit("vector", lambda v: v.tensor_scalar(SM2[:, 192:192 + ngrp],
                                                     SM2[:, 128:128 + ngrp], -1.0, None,
                                                     op0=ALU.mult),
                 reads=["q_thr"], writes=["q_nthr"])
            t3 = tq_buf.rearrange("p (g w) -> p g w", w=128)
            emit("vector", lambda v: v.tensor_tensor(
                t3, w3, SM2[:, 128:128 + ngrp].rearrange("p (g o) -> p g o", o=1)
                .broadcast_to([128, ngrp, 128]), op=ALU.is_ge),
                reads=[key_w, "q_thr"], writes=[key_tq])
            emit("vector", lambda v: v.tensor_tensor(
                w3, w3, SM2[:, 192:192 + ngrp].rearrange("p (g o) -> p g o", o=1)
                .broadcast_to([128, ngrp, 128]), op=ALU.is_le),
                reads=[key_w, "q_nthr"], writes=[key_w])
            emit("vector", lambda v: v.tensor_sub(tq_buf, tq_buf, wbuf),
                 reads=[key_tq, key_w], writes=[key_tq])
            emit("vector", lambda v: v.tensor_tensor(
                t3, t3, SM2[:, 64:64 + ngrp].rearrange("p (g o) -> p g o", o=1)
                .broadcast_to([128, ngrp, 128]), op=ALU.mult),
                reads=[key_tq, "q_sc"], writes=[key_tq])

        def hilo_split(tq_buf, key_tq, hi_buf, key_hi, lo_buf, key_lo, ngrp):
            n = ngrp * 128
            T.emit("vector", lambda v: v.tensor_copy(hi_buf[:, 0:n], tq_buf[:, 0:n]),
                   reads=[key_tq], writes=[key_hi])
            T.emit("vector", lambda v: v.tensor_sub(lo_buf[:, 0:n], tq_buf[:, 0:n],
                                                    hi_buf[:, 0:n]),
                   reads=[key_tq, key_hi], writes=[key_lo])

        AR2_ctx = nc.sbuf_tensor("arena2", [128, 22528], F32)
        AR2 = AR2_ctx.__enter__()
        ych = [(AR1[:, 2048 * c:2048 * (c + 1)] if c < 12
                else AR2[:, 2048 * (c - 12):2048 * (c - 11)]) for c in range(NDC)]
        yq4 = [(AR1[:, 8192 * q:8192 * (q + 1)] if q < 3 else AR2[:, 0:8192])
               .rearrange("p (c n) -> p c n", c=4) for q in range(4)]
        xt = AR2[:, 8192:16384].rearrange("p (c n) -> p c n", c=4)
        stg = [AR2[:, 16384 + 2048 * i:16384 + 2048 * (i + 1)] for i in range(2)]

        for ti in range(4):
            tok0 = 512 * ti
            T.dma("sync", xt[:, :, :],
                  x_h[tok0:tok0 + 512, :].rearrange("(c p) n -> p c n", p=128),
                  writes=["xt"], tag=f"ldx{ti}")
            for cc in range(4):
                xc = xt[:, cc, :]
                sb = cc % 2
                T.emit("vector", lambda v, xc=xc: v.reduce_sum(
                    SMALL[:, 0:1], xc, axis=AX.X), reads=["xt"], writes=["s_sum"])
                T.emit("scalar", lambda s, xc=xc, sb=sb: s.activation(
                    stg[sb][:], xc, AF.Square, accum_out=SMALL[:, 1:2]),
                    reads=["xt"], writes=[f"stg{sb}", "s_sq"])
                T.emit("vector", lambda v: v.tensor_scalar(
                    SMALL[:, 2:3], SMALL[:, 0:1], 1.0 / DM, None, op0=ALU.mult),
                    reads=["s_sum"], writes=["s_m"])
                T.emit("vector", lambda v: v.tensor_scalar(
                    SMALL[:, 3:4], SMALL[:, 2:3], SMALL[:, 2:3], None, op0=ALU.mult),
                    reads=["s_m"], writes=["s_m2"])
                T.emit("vector", lambda v: v.tensor_scalar(
                    SMALL[:, 4:5], SMALL[:, 1:2], 1.0 / DM, SMALL[:, 3:4],
                    op0=ALU.mult, op1=ALU.subtract),
                    reads=["s_sq", "s_m2"], writes=["s_v"])
                T.emit("scalar", lambda s: s.activation(
                    SMALL[:, 5:6], SMALL[:, 4:5], AF.Sqrt, bias=EPSC),
                    reads=["s_v", "epsc"], writes=["s_sq2"])
                T.emit("vector", lambda v: v.reciprocal(SMALL[:, 6:7], SMALL[:, 5:6]),
                       reads=["s_sq2"], writes=["s_r0"])
                T.emit("vector", lambda v: v.tensor_scalar(
                    SMALL[:, 7:8], SMALL[:, 6:7], SMALL[:, 6:7], None, op0=ALU.mult),
                    reads=["s_r0"], writes=["s_rr"])
                T.emit("vector", lambda v: v.tensor_scalar(
                    SMALL[:, 8:9], SMALL[:, 4:5], 0.5, 0.5 * EPS,
                    op0=ALU.mult, op1=ALU.add), reads=["s_v"], writes=["s_hv"])
                T.emit("vector", lambda v: v.tensor_scalar(
                    SMALL[:, 9:10], SMALL[:, 7:8], SMALL[:, 8:9], 1.5,
                    op0=ALU.mult, op1=ALU.subtract), reads=["s_rr", "s_hv"],
                    writes=["s_t"])
                T.emit("vector", lambda v: v.tensor_scalar(
                    SMALL[:, 9:10], SMALL[:, 9:10], -1.0, None, op0=ALU.mult),
                    reads=["s_t"], writes=["s_t"])
                T.emit("vector", lambda v: v.tensor_scalar(
                    SMALL[:, 10:11], SMALL[:, 6:7], SMALL[:, 9:10], None, op0=ALU.mult),
                    reads=["s_r0", "s_t"], writes=["s_r1"])
                T.emit("vector", lambda v, xc=xc: v.tensor_scalar(
                    xc, xc, SMALL[:, 2:3], SMALL[:, 10:11],
                    op0=ALU.subtract, op1=ALU.mult),
                    reads=["xt", "s_m", "s_r1"], writes=["xt"])
                tok = tok0 + cc * 128
                for fq in range(4):
                    for ff in range(4):
                        fc = fq * 4 + ff
                        T.emit("tensor", lambda t, xc=xc, fc=fc, ff=ff: t.transpose(
                            psb(0)[:, 128 * ff:128 * (ff + 1)],
                            xc[:, 128 * fc:128 * (fc + 1)], ident),
                            reads=["xt", "co"], writes=["psB0"])
                    T.emit("vector", lambda v, fq=fq, tok=tok: v.tensor_copy(
                        yq4[fq][:, 0:4, tok:tok + 128],
                        psb(0)[:, 0:512].rearrange("p (c n) -> p c n", c=4)),
                        reads=["psB0"], writes=["y"])

        wbuf = [AR2[:, 8192 + 4096 * i:8192 + 4096 * (i + 1)]
                .rearrange("p (c n) -> p c n", c=NDC) for i in range(2)]
        for wi, (w_hh, scr, nm) in enumerate(((wq_h, qs, "q"), (wk_h, ks, "k"),
                                              (wv_h, vs, "v"))):
            for dh in range(8):
                bi = (wi * 8 + dh) % 2
                al = ["xt"] if (wi == 0 and dh < 2) else []
                T.dma("sync", wbuf[bi][:, :, :],
                      w_hh[:, 256 * dh:256 * (dh + 1)].rearrange("(c p) n -> p c n", p=128),
                      writes=[f"wb{bi}"] + al, tag=f"ldw{nm}{dh}")
                for sub in range(2):
                    dc = dh * 2 + sub
                    for ns in range(4):
                        for din in range(NDC):
                            T.emit("tensor", lambda t, bi=bi, sub=sub, ns=ns, din=din: t.matmul(
                                PSA[:, 512 * ns:512 * (ns + 1)],
                                wbuf[bi][:, din, 128 * sub:128 * (sub + 1)],
                                ych[din][:, 512 * ns:512 * (ns + 1)],
                                start=(din == 0), stop=(din == NDC - 1)),
                                reads=[f"wb{bi}", "y"], writes=["psA"], tag=f"mm{nm}{dc}")
                    sb = dc % 2
                    if nm == "q":
                        T.emit("vector", lambda v, sb=sb: v.tensor_scalar(
                            stg[sb][:], PSA[:], SM, None, op0=ALU.mult),
                            reads=["psA"], writes=[f"stg{sb}"])
                    else:
                        T.emit("vector", lambda v, sb=sb: v.tensor_copy(stg[sb][:], PSA[:]),
                               reads=["psA"], writes=[f"stg{sb}"])
                    if nm == "k":
                        T.emit("vector", lambda v, sb=sb, dc=dc: v.reduce_sum(
                            KG[:, NB * dc:NB * (dc + 1)],
                            stg[sb][:].rearrange("p (b w) -> p b w", w=64), axis=AX.X),
                            reads=[f"stg{sb}"], writes=["kg"])
                    T.dma("sync", scr[dc, :, :], stg[sb][:], reads=[f"stg{sb}"],
                          writes=[f"{nm}sc{dc}"], tag=f"st{nm}{dc}")

        dbg_dump(T, ["qs", "ks", "vs"])
        T.phase()

        qh = AR1[:, 0:2048]
        kh = [AR1[:, 2048:4096], AR1[:, 4096:6144]]
        vh = [AR1[:, 6144:8192], AR1[:, 8192:10240]]
        vt = AR1[:, 10240:12288]
        Pt = AR1[:, 12288:14336]
        PT = AR1[:, 14336:22528].rearrange("p (c n) -> p c n", c=16)
        oh = AR1[:, 22528:24576]
        wdl = [AR2[:, 5504 * i:5504 * (i + 1)] for i in range(2)]
        Pt2 = AR2[:, 11008:13056]
        wdq = AR2[:, 13056:18560]
        Pt3 = AR2[:, 18560:20608]
        Pts = [Pt, Pt2, Pt3]

        pending = []
        for h in range(H):
            b = h % 2
            with T.dma_batch("sync"):
                T.dma("sync", qh[:], qs[h, :, :], reads=[f"qsc{h}"], writes=["qh"],
                      tag=f"ldq{h}")
                T.dma("sync", kh[b][:], ks[h, :, :], reads=[f"ksc{h}"], writes=[f"kh{b}"],
                      tag=f"ldk{h}")
                T.dma("sync", vh[b][:], vs[h, :, :], reads=[f"vsc{h}"], writes=[f"vh{b}"],
                      tag=f"ldv{h}")
            al = ["y"] if h < 2 else []
            T.dma("sync", wdl[b][:], wd_h[128 * h:128 * (h + 1), :],
                  writes=[f"wdl{b}"] + al, tag=f"ldwd{h}")
            for j in range(16):
                T.emit("tensor", lambda t, j=j, h=h: t.matmul(
                    psb(0)[:, 32 * j:32 * (j + 1)], qh[:, 128 * j:128 * (j + 1)],
                    KG[:, NB * h:NB * (h + 1)], start=(j == 0), stop=(j == 15)),
                    reads=["qh", "kg"], writes=["psB0"], tag=f"gate{h}")
            for j in range(16):
                T.emit("vector", lambda v, j=j: v.scalar_tensor_tensor(
                    GW[:, 32 * j:32 * (j + 1)], psb(0)[:, 32 * j:32 * (j + 1)], 1.0 / 64,
                    CO[:, 32 - 2 * j:64 - 2 * j], op0=ALU.mult, op1=ALU.add),
                    reads=["psB0", "co"], writes=["gw"])
                T.emit("vector", lambda v, j=j: v.max(
                    T8[:, 8 * j:8 * (j + 1)], GW[:, 32 * j:32 * (j + 1)]),
                    reads=["gw"], writes=["t8"])
                T.emit("vector", lambda v, j=j: v.tensor_scalar(
                    BB[:, 32 * j:32 * (j + 1)], GW[:, 32 * j:32 * (j + 1)],
                    T8[:, 8 * j + 7:8 * j + 8], -BIG, op0=ALU.is_lt, op1=ALU.mult),
                    reads=["gw", "t8"], writes=["bb"])
            for kc4 in range(4):
                for ff in range(4):
                    kc = kc4 * 4 + ff
                    T.emit("tensor", lambda t, b=b, kc=kc, ff=ff: t.transpose(
                        psb(1)[:, 128 * ff:128 * (ff + 1)],
                        vh[b][:, 128 * kc:128 * (kc + 1)], ident),
                        reads=[f"vh{b}", "co"], writes=["psB1"])
                T.emit("scalar", lambda s, kc4=kc4: s.activation(
                    vt[:, 512 * kc4:512 * (kc4 + 1)], psb(1)[:, 0:512], AF.Copy),
                    reads=["psB1"], writes=["vt"])
            slice_ctr = [0]

            def emit_A(j):
                kend = 128 * (j + 1)
                par = j % 3
                ptb = Pts[par]
                nsl = list(range(0, kend, 512))
                for n0 in nsl:
                    n1 = min(kend, n0 + 512)
                    bk = slice_ctr[0] % 4
                    slice_ctr[0] += 1
                    last = (n0 == nsl[-1])
                    psl = PSA[:, 512 * bk:512 * bk + (n1 - n0)]
                    T.emit("tensor", lambda t, b=b, j=j, n0=n0, n1=n1, psl=psl,
                           last=last: t.matmul(
                        psl, qh[:, 128 * j:128 * (j + 1)], kh[b][:, n0:n1],
                        start=True, stop=not last),
                        reads=["qh", f"kh{b}"], writes=[f"psA{bk}"], tag=f"sc{h}_{j}")
                    if last:
                        w = n1 - n0
                        T.emit("tensor", lambda t, psl=psl, w=w: t.matmul(
                            psl[:, w - 128:w], ident, causal,
                            start=False, stop=True),
                            reads=["co"], writes=[f"psA{bk}"], tag=f"cz{h}_{j}")
                    nb0 = n0 // 64
                    nbn = (n1 - n0) // 64
                    T.emit("vector", lambda v, j=j, n0=n0, n1=n1, psl=psl, par=par,
                           ptb=ptb, nb0=nb0, nbn=nbn: v.tensor_tensor(
                        ptb[:, n0:n1].rearrange("p (b w) -> p b w", w=64),
                        psl.rearrange("p (b w) -> p b w", w=64),
                        BB[:, 32 * j + nb0:32 * j + nb0 + nbn]
                        .rearrange("p (b o) -> p b o", o=1)
                        .broadcast_to([128, nbn, 64]),
                        op=ALU.add), reads=[f"psA{bk}", "bb"], writes=[f"Pt{par}"])
                T.emit("vector", lambda v, kend=kend, par=par, ptb=ptb: v.reduce_max(
                    SMALL[:, 20 + 3 * par:21 + 3 * par], ptb[:, 0:kend], axis=AX.X,
                    negate=True), reads=[f"Pt{par}"], writes=[f"a_m{par}"])
                T.emit("scalar", lambda s, kend=kend, par=par, ptb=ptb: s.activation(
                    ptb[:, 0:kend], ptb[:, 0:kend], AF.Exp,
                    bias=SMALL[:, 20 + 3 * par:21 + 3 * par], scale=1.0,
                    accum_out=SMALL[:, 21 + 3 * par:22 + 3 * par]),
                    reads=[f"Pt{par}", f"a_m{par}"],
                    writes=[f"Pt{par}", f"a_sig{par}"])

            def emit_B(jb):
                kend = 128 * (jb + 1)
                par = jb % 3
                jjb = jb % 4
                ptb = Pts[par]
                T.emit("vector", lambda v, par=par: v.reciprocal(
                    SMALL[:, 22 + 3 * par:23 + 3 * par],
                    SMALL[:, 21 + 3 * par:22 + 3 * par]),
                    reads=[f"a_sig{par}"], writes=[f"a_inv{par}"])
                T.emit("vector", lambda v, kend=kend, par=par, ptb=ptb: v.tensor_scalar(
                    ptb[:, 0:kend], ptb[:, 0:kend],
                    SMALL[:, 22 + 3 * par:23 + 3 * par], None, op0=ALU.mult),
                    reads=[f"Pt{par}", f"a_inv{par}"], writes=[f"Pt{par}"])
                for kc0 in range(0, jb + 1, 4):
                    kcn = min(4, jb + 1 - kc0)
                    bk = 2 + ((kc0 // 4) % 2)
                    for ff in range(kcn):
                        kc = kc0 + ff
                        T.emit("tensor", lambda t, kc=kc, ff=ff, ptb=ptb, bk=bk: t.transpose(
                            psb(bk)[:, 128 * ff:128 * (ff + 1)],
                            ptb[:, 128 * kc:128 * (kc + 1)], ident),
                            reads=[f"Pt{par}", "co"], writes=[f"psB{bk}"])
                    T.emit("scalar", lambda s, kc0=kc0, kcn=kcn, jjb=jjb, bk=bk: s.activation(
                        PT[:, kc0:kc0 + kcn, 128 * jjb:128 * (jjb + 1)],
                        psb(bk)[:, 0:128 * kcn].rearrange("p (c n) -> p c n", c=kcn),
                        AF.Copy),
                        reads=[f"psB{bk}"], writes=["PT"])

            def emit_PV(g):
                nkc_g = 4 * g + 4
                for kc in range(nkc_g):
                    col0 = 128 * max(0, kc - 4 * g)
                    T.emit("tensor", lambda t, kc=kc, col0=col0, nkc_g=nkc_g: t.matmul(
                        psb(3)[:, col0:512], vt[:, 128 * kc:128 * (kc + 1)],
                        PT[:, kc, col0:512],
                        start=(kc == 0), stop=(kc == nkc_g - 1)),
                        reads=["vt", "PT"], writes=["psB3"], tag=f"pv{h}_{g}")
                T.emit("scalar", lambda s, g=g: s.activation(
                    oh[:, 512 * g:512 * (g + 1)], psb(3)[:, 0:512], AF.Copy),
                    reads=["psB3"], writes=["oh"])
                T.dma("sync", os_[g, h, :, :], oh[:, 512 * g:512 * (g + 1)],
                      reads=["oh"], writes=[f"ossc{g}_{h}"], tag=f"sto{h}_{g}")

            for j in range(16):
                jb = j - 2
                if jb >= 0:
                    emit_B(jb)
                    if jb % 4 == 3:
                        emit_PV(jb // 4)
                if pending:
                    pending.pop(0)()
                emit_A(j)
                if pending:
                    pending.pop(0)()
            emit_B(14)
            emit_B(15)
            emit_PV(3)
            while pending:
                pending.pop(0)()
            dc = h
            ternary_quant(wdl[b], f"wdl{b}", wdq, "wdq", NFC, sink=pending)

            def wd_evac(fc0, dc=dc):
                fcn = min(4, NFC - fc0)
                for ff in range(fcn):
                    fc = fc0 + ff
                    T.emit("tensor", lambda t, fc=fc, ff=ff: t.transpose(
                        psb(1)[:, 128 * ff:128 * (ff + 1)],
                        wdq[:, 128 * fc:128 * (fc + 1)], ident),
                        reads=["wdq", "co"], writes=["psB1"])
                T.emit("scalar", lambda s, fcn=fcn: s.activation(
                    WDTBH[:, 0:128 * fcn], psb(1)[:, 0:128 * fcn], AF.Copy),
                    reads=["psB1"], writes=["wdtbh"])
                T.emit("vector", lambda v, fcn=fcn: v.tensor_sub(
                    WDTBL[:, 0:128 * fcn], psb(1)[:, 0:128 * fcn], WDTBH[:, 0:128 * fcn]),
                    reads=["psB1", "wdtbh"], writes=["wdtbl"])
                T.dma("sync", wds[dc, 0, fc0:fc0 + fcn, :, :].rearrange("c p n -> p c n"),
                      WDTBH[:, 0:128 * fcn].rearrange("p (c n) -> p c n", c=fcn),
                      reads=["wdtbh"], writes=[f"wdsc{dc}"], tag=f"stwdh{dc}_{fc0}")
                T.dma("sync", wds[dc, 1, fc0:fc0 + fcn, :, :].rearrange("c p n -> p c n"),
                      WDTBL[:, 0:128 * fcn].rearrange("p (c n) -> p c n", c=fcn),
                      reads=["wdtbl"], writes=[f"wdsc{dc}"], tag=f"stwdl{dc}_{fc0}")

            for fc0 in range(0, NFC, 4):
                pending.append(lambda fc0=fc0: wd_evac(fc0))
        while pending:
            pending.pop(0)()

        dbg_dump(T, ["os", "wds"])
        T.phase()

        osg = AR1[:, 0:8192].rearrange("p (c n) -> p c n", c=16)
        xtm = AR1[:, 8192:16384].rearrange("p (c n) -> p c n", c=4)
        xtmf = AR1[:, 8192:16384]
        xfm2 = [AR1[:, 16384:24576].rearrange("p (c n) -> p c n", c=16),
                AR2[:, 12288:20480].rearrange("p (c n) -> p c n", c=16)]
        xfm2f = [AR1[:, 16384:24576], AR2[:, 12288:20480]]
        x1g = AR2[:, 0:8192].rearrange("p (c n) -> p c n", c=16)
        woc = [AR2[:, 8192 + 2048 * i:8192 + 2048 * (i + 1)]
               .rearrange("p (c n) -> p c n", c=16) for i in range(2)]
        SR3 = AR2[:, 20480:22528]

        def p3_xload(g):
            par = g % 2
            xf = xfm2[par]
            T.dma("sync", xtm[:, :, :],
                  x_h[512 * g:512 * g + 512, :].rearrange("(c p) n -> p c n", p=128),
                  writes=["xtm"], tag=f"ldx2_{g}")
            for cc in range(4):
                for fq in range(4):
                    for ff in range(4):
                        fc = fq * 4 + ff
                        T.emit("tensor", lambda t, cc=cc, fc=fc, ff=ff: t.transpose(
                            psb(0)[:, 128 * ff:128 * (ff + 1)],
                            xtm[:, cc, 128 * fc:128 * (fc + 1)], ident),
                            reads=["xtm", "co"], writes=["psB0"])
                    T.emit("vector", lambda v, cc=cc, fq=fq, xf=xf: v.tensor_copy(
                        xf[:, 4 * fq:4 * fq + 4, 128 * cc:128 * (cc + 1)],
                        psb(0)[:, 0:512].rearrange("p (c n) -> p c n", c=4)),
                        reads=["psB0"],
                        writes=[f"xf{par}_{4 * fq + i}" for i in range(4)])

        def p3_store(gq):
            parq = gq % 2
            with T.dma_batch("gpsimd"):
                for dc in range(NDC):
                    T.dma("gpsimd", xqs[gq, dc, :, :], xfm2[parq][:, dc, :],
                          reads=[f"xf{parq}_{dc}"], writes=[f"xqsc{gq}"],
                          tag=f"stxq{gq}_{dc}")

        T.dma("sync", osg[:, :, :], os_[0, :, :, :].rearrange("h p n -> p h n"),
              reads=[f"ossc0_{hh}" for hh in range(H)], writes=["osg"], tag="ldo0")
        p3_xload(0)
        for g in range(4):
            par = g % 2
            parq = 1 - par
            xf = xfm2[par]
            for grp in range(4):
                if g > 0:
                    ladder(xfm2f[parq][:, 2048 * grp:2048 * (grp + 1)],
                           xtmf[:, 0:4096],
                           [f"xf{parq}_{4 * grp + i}" for i in range(4)],
                           2048, tmp_key="xtm")
                for dc in range(4 * grp, 4 * grp + 4):
                    wb = dc % 2
                    pa = dc % 2
                    T.dma("sync", woc[wb][:, :, :],
                          wo_h[:, 128 * dc:128 * (dc + 1)]
                          .rearrange("(c p) n -> p c n", p=128),
                          writes=[f"woc{wb}"], tag=f"ldwo{g}_{dc}")
                    for hd in range(H):
                        T.emit("tensor", lambda t, wb=wb, hd=hd, pa=pa: t.matmul(
                            psb(pa)[:, 0:512], woc[wb][:, hd, :], osg[:, hd, :],
                            start=(hd == 0), stop=(hd == H - 1)),
                            reads=[f"woc{wb}", "osg"], writes=[f"psB{pa}"],
                            tag=f"wo{g}_{dc}")
                    T.emit("vector", lambda v, dc=dc, pa=pa, xf=xf: v.tensor_add(
                        x1g[:, dc, :], psb(pa)[:, 0:512], xf[:, dc, :]),
                        reads=[f"psB{pa}", f"xf{par}_{dc}"], writes=[f"x1g_{dc}"])
                    T.emit("tensor", lambda t, dc=dc: t.matmul(
                        psb(2)[:, 0:512], ONES[:], x1g[:, dc, :],
                        start=(dc == 0), stop=(dc == NDC - 1)),
                        reads=["ones", f"x1g_{dc}"], writes=["psB2"], tag=f"lns{g}")
                    T.emit("scalar", lambda s, dc=dc, xf=xf: s.activation(
                        xf[:, dc, :], x1g[:, dc, :], AF.Square),
                        reads=[f"x1g_{dc}"], writes=[f"xf{par}_{dc}"])
                    T.emit("tensor", lambda t, dc=dc, xf=xf: t.matmul(
                        psb(3)[:, 0:512], ONES[:], xf[:, dc, :],
                        start=(dc == 0), stop=(dc == NDC - 1)),
                        reads=["ones", f"xf{par}_{dc}"], writes=["psB3"],
                        tag=f"lnsq{g}")
                    T.dma("sync", x1s[g, dc, :, :], x1g[:, dc, :],
                          reads=[f"x1g_{dc}"], writes=[f"x1sc{g}"],
                          tag=f"stx1_{g}_{dc}")
            if g > 0:
                p3_store(g - 1)
            if g < 3:
                T.dma("sync", osg[:, :, :],
                      os_[g + 1, :, :, :].rearrange("h p n -> p h n"),
                      reads=[f"ossc{g + 1}_{hh}" for hh in range(H)],
                      writes=["osg"], tag=f"ldo{g + 1}")
            T.emit("vector", lambda v: v.tensor_scalar(
                MREP[:], psb(2)[:, 0:512], 1.0 / DM, None, op0=ALU.mult),
                reads=["psB2"], writes=["mrep"])
            T.emit("vector", lambda v: v.tensor_mul(SR3[:, 0:512], MREP[:], MREP[:]),
                   reads=["mrep"], writes=["sr3a"])
            T.emit("vector", lambda v: v.scalar_tensor_tensor(
                RREP[:], psb(3)[:, 0:512], 1.0 / DM, SR3[:, 0:512],
                op0=ALU.mult, op1=ALU.subtract),
                reads=["psB3", "sr3a"], writes=["rrep"])
            if probe_ap is not None:
                T.dma("gpsimd", probe_ap[g, :, 1024:1536], RREP[:], reads=["rrep"], tag="pbv")
                T.dma("gpsimd", probe_ap[g, :, 1536:2048], SR3[:, 0:512], reads=["sr3a"], tag="pbm2")
            T.emit("scalar", lambda s: s.activation(RREP[:], RREP[:], AF.Sqrt, bias=EPSC),
                   reads=["rrep", "epsc"], writes=["rrep"])
            T.emit("vector", lambda v: v.reciprocal(SR3[:, 0:512], RREP[:]),
                   reads=["rrep"], writes=["sr3a"])
            T.emit("vector", lambda v: v.tensor_mul(SR3[:, 512:1024], RREP[:], SR3[:, 0:512]),
                   reads=["rrep", "sr3a"], writes=["p3u"])
            T.emit("vector", lambda v: v.tensor_mul(SR3[:, 1024:1536], SR3[:, 512:1024],
                                                    SR3[:, 512:1024]),
                   reads=["p3u"], writes=["p3t"])
            T.emit("vector", lambda v: v.tensor_scalar(
                SR3[:, 1024:1536], SR3[:, 1024:1536], -0.5, 1.5, op0=ALU.mult, op1=ALU.add),
                reads=["p3t"], writes=["p3t"])
            T.emit("vector", lambda v: v.tensor_mul(RREP[:], SR3[:, 0:512], SR3[:, 1024:1536]),
                   reads=["sr3a", "p3t"], writes=["rrep"])
            if probe_ap is not None:
                T.dma("gpsimd", probe_ap[g, :, 0:512], MREP[:], reads=["mrep"], tag="pb1")
                T.dma("gpsimd", probe_ap[g, :, 512:1024], RREP[:], reads=["rrep"], tag="pb2")
            T.emit("vector", lambda v: v.memset(SR3[:, 1536:2048], 0.0), writes=["p3m16"])
            T.emit("vector", lambda v: v.memset(SR3[:, 512:1024], 0.0), writes=["p3u"])
            for dc in range(NDC):
                T.emit("vector", lambda v, dc=dc, xf=xf: v.tensor_sub(
                    xf[:, dc, :], x1g[:, dc, :], MREP[:]),
                    reads=[f"x1g_{dc}", "mrep"], writes=[f"xf{par}_{dc}"])
                T.emit("vector", lambda v, dc=dc, xf=xf: v.tensor_mul(
                    xf[:, dc, :], xf[:, dc, :], RREP[:]),
                    reads=[f"xf{par}_{dc}", "rrep"], writes=[f"xf{par}_{dc}"])
                T.emit("vector", lambda v, dc=dc, xf=xf: v.tensor_max(
                    SR3[:, 1536:2048], SR3[:, 1536:2048], xf[:, dc, :]),
                    reads=[f"xf{par}_{dc}", "p3m16"], writes=["p3m16"])
                T.emit("vector", lambda v, dc=dc, xf=xf: v.tensor_tensor(
                    SR3[:, 512:1024], SR3[:, 512:1024], xf[:, dc, :], op=ALU.min),
                    reads=[f"xf{par}_{dc}", "p3u"], writes=["p3u"])
            T.emit("vector", lambda v: v.tensor_scalar(
                SR3[:, 512:1024], SR3[:, 512:1024], -1.0, None, op0=ALU.mult),
                reads=["p3u"], writes=["p3u"])
            T.emit("vector", lambda v: v.tensor_max(
                SR3[:, 1536:2048], SR3[:, 1536:2048], SR3[:, 512:1024]),
                reads=["p3u", "p3m16"], writes=["p3m16"])
            for ts in range(4):
                T.emit("tensor", lambda t, ts=ts: t.transpose(
                    psb(0)[:, 0:128], SR3[:, 1536 + 128 * ts:1536 + 128 * (ts + 1)], ident),
                    reads=["p3m16", "co"], writes=["psB0"])
                T.emit("vector", lambda v, ts=ts: v.reduce_max(
                    SM2[:, ts:ts + 1], psb(0)[:, 0:128], axis=AX.X),
                    reads=["psB0"], writes=["amax1"])
            T.emit("vector", lambda v: v.tensor_scalar(
                SM2[:, 0:4], SM2[:, 0:4], EPS, None, op0=ALU.max),
                reads=["amax1"], writes=["amax1"])
            T.emit("tensor", lambda t: t.transpose(psb(1)[:, 0:128], SM2[:, 0:128], ident),
                   reads=["amax1", "co"], writes=["psB1"])
            T.emit("vector", lambda v: v.tensor_copy(SM2[:, 128:256], psb(1)[:, 0:128]),
                   reads=["psB1"], writes=["amaxT"])
            with T.dma_batch("gpsimd"):
                for c in range(4):
                    T.dma("gpsimd", SR3[0:1, 128 * c:128 * (c + 1)], SM2[c:c + 1, 128:256],
                          reads=["amaxT"], writes=["sr3a"], tag=f"af{g}_{c}")
            T.emit("tensor", lambda t: t.matmul(
                psb(1)[:, 0:512], ONES[0:1, :], SR3[0:1, 0:512], start=True, stop=True),
                reads=["sr3a", "ones"], writes=["psB1"], tag=f"rep{g}")
            T.emit("vector", lambda v: v.tensor_scalar(
                MREP[:], psb(1)[:, 0:512], 1.0 / 127.0, None, op0=ALU.mult),
                reads=["psB1"], writes=["mrep"])
            T.emit("vector", lambda v: v.reciprocal(RREP[:], MREP[:]),
                   reads=["mrep"], writes=["rrep"])
            T.dma("sync", ss1[g, :, :], MREP[:], reads=["mrep"], writes=[f"ss1c{g}"],
                  tag=f"stss1_{g}")
            for dc in range(NDC):
                T.emit("vector", lambda v, dc=dc, xf=xf: v.tensor_mul(
                    xf[:, dc, :], xf[:, dc, :], RREP[:]),
                    reads=[f"xf{par}_{dc}", "rrep"], writes=[f"xf{par}_{dc}"])
            if g < 3:
                p3_xload(g + 1)
        for grp in range(4):
            ladder(xfm2f[1][:, 2048 * grp:2048 * (grp + 1)], xtmf[:, 0:4096],
                   [f"xf1_{4 * grp + i}" for i in range(4)], 2048, tmp_key="xtm")
        p3_store(3)

        dbg_dump(T, ["x1s", "xqs"])
        T.phase()

        AR2_ctx.__exit__(None, None, None)
        BFB_ctx = nc.sbuf_tensor("bfarena", [128, 44032], BF16)
        BFB = BFB_ctx.__enter__()
        wl4 = [AR1[:, 2048 * i:2048 * (i + 1)] for i in range(2)]
        tq4 = [AR1[:, 4096 + 2048 * i:4096 + 2048 * (i + 1)] for i in range(2)]
        wqT32g = AR1[:, 8192:10240]
        wqT32u = AR1[:, 10240:12288]
        gs_b = AR1[:, 12288:14336]
        hb = AR1[:, 14336:16384]
        M43 = AR1[:, 16384:18432]
        is1 = AR1[:, 18432:20480]
        M43N = AR1[:, 20480:22528]
        xqb = BFB[:, 0:32768].rearrange("p (c n) -> p c n", c=NDC)
        wqbg = [BFB[:, 32768 + 2048 * i:32768 + 2048 * (i + 1)] for i in range(2)]
        wqbu = [BFB[:, 36864 + 2048 * i:36864 + 2048 * (i + 1)] for i in range(2)]

        with T.dma_batch("sync"):
            for g in range(4):
                for dc in range(NDC):
                    T.dma("sync", xqb[:, dc, 512 * g:512 * (g + 1)], xqs[g, dc, :, :],
                          reads=[f"xqsc{g}"], writes=["xqb"], tag=f"ldxq{g}_{dc}")
        T.emit("vector", lambda v: v.memset(M43[:], 0.0),
               writes=["m43h0", "m43h1"])
        T.emit("vector", lambda v: v.memset(M43N[:], 0.0),
               writes=["m43n0", "m43n1"])
        with T.dma_batch("sync"):
            for g in range(4):
                T.dma("sync", is1[:, 512 * g:512 * (g + 1)], ss1[g, :, :],
                      reads=[f"ss1c{g}"], writes=["is1"], tag=f"ldss1_{g}")

        pend4 = []

        def gateup_quant(w_hh, fc, par, dbkey):
            T.dma("sync", wl4[par][:], w_hh[128 * fc:128 * (fc + 1), :],
                  writes=[f"wl4{par}"], tag=f"ld{dbkey}{fc}")
            ternary_quant(wl4[par], f"wl4{par}", tq4[par], f"tq4{par}", NDC,
                          sink=pend4)

        def gateup_finish(par, dst32, d32key, dstb, dbkey, banks):
            for i4, c0 in enumerate(range(0, 16, 4)):
                bk = banks[i4 % 2]
                for ff in range(4):
                    c = c0 + ff
                    T.emit("tensor", lambda t, c=c, ff=ff, par=par, bk=bk: t.transpose(
                        psb(bk)[:, 128 * ff:128 * (ff + 1)],
                        tq4[par][:, 128 * c:128 * (c + 1)], ident),
                        reads=[f"tq4{par}", "co"], writes=[f"psB{bk}"])
                T.emit("scalar", lambda s, c0=c0, dst32=dst32, bk=bk: s.activation(
                    dst32[:, 128 * c0:128 * c0 + 512], psb(bk)[:, 0:512], AF.Copy),
                    reads=[f"psB{bk}"], writes=[d32key])
            T.emit("vector", lambda v, dst32=dst32, dstb=dstb: v.tensor_copy(
                dstb[0][:], dst32[:]), reads=[d32key], writes=[dbkey + "h"])
            T.emit("vector", lambda v, dst32=dst32, dstb=dstb: v.tensor_sub(
                dstb[1][:], dst32[:], dstb[0][:]),
                reads=[d32key, dbkey + "h"], writes=[dbkey + "l"])

        def pop4(n=2):
            for _ in range(n):
                if pend4:
                    pend4.pop(0)()

        gateup_quant(wg_h, 0, 0, "wqbg")
        gateup_quant(wu_h, 0, 1, "wqbu")
        while pend4:
            pend4.pop(0)()
        gateup_finish(0, wqT32g, "wqT32g", wqbg, "wqbg", (0, 1))
        gateup_finish(1, wqT32u, "wqT32u", wqbu, "wqbu", (2, 3))
        for fc in range(NFC):
            if fc + 1 < NFC:
                gateup_quant(wg_h, fc + 1, 0, "wqbg")
                gateup_quant(wu_h, fc + 1, 1, "wqbu")
            for nh in range(2):
                for sl in range(2):
                    ns = 2 * nh + sl
                    for din in range(NDC):
                        for hl in range(2):
                            T.emit("tensor", lambda t, sl=sl, ns=ns, din=din, hl=hl: t.matmul(
                                PSA[:, 512 * sl:512 * (sl + 1)],
                                wqbg[hl][:, 128 * din:128 * (din + 1)],
                                xqb[:, din, 512 * ns:512 * (ns + 1)],
                                start=(din == 0 and hl == 0),
                                stop=(din == NDC - 1 and hl == 1)),
                                reads=["wqbgh", "wqbgl", "xqb"], writes=["psAg"],
                                tag=f"gmm{fc}")
                for sl in range(2):
                    ns = 2 * nh + sl
                    for din in range(NDC):
                        for hl in range(2):
                            T.emit("tensor", lambda t, sl=sl, ns=ns, din=din, hl=hl: t.matmul(
                                PSA[:, 1024 + 512 * sl:1024 + 512 * (sl + 1)],
                                wqbu[hl][:, 128 * din:128 * (din + 1)],
                                xqb[:, din, 512 * ns:512 * (ns + 1)],
                                start=(din == 0 and hl == 0),
                                stop=(din == NDC - 1 and hl == 1)),
                                reads=["wqbuh", "wqbul", "xqb"], writes=["psAu"],
                                tag=f"umm{fc}")
                h0 = 1024 * nh
                T.emit("vector", lambda v, h0=h0: v.tensor_mul(
                    gs_b[:, h0:h0 + 1024], PSA[:, 0:1024], is1[:, h0:h0 + 1024]),
                    reads=["psAg", "is1"], writes=[f"gs{nh}"])
                pop4()
                T.emit("scalar", lambda s, h0=h0: s.activation(
                    gs_b[:, h0:h0 + 1024], gs_b[:, h0:h0 + 1024], AF.Silu),
                    reads=[f"gs{nh}"], writes=[f"gs{nh}"])
                T.emit("vector", lambda v, h0=h0: v.tensor_mul(
                    hb[:, h0:h0 + 1024], PSA[:, 1024:2048], is1[:, h0:h0 + 1024]),
                    reads=["psAu", "is1"], writes=[f"hb{nh}"])
                pop4()
                T.emit("vector", lambda v, h0=h0: v.tensor_mul(
                    hb[:, h0:h0 + 1024], gs_b[:, h0:h0 + 1024], hb[:, h0:h0 + 1024]),
                    reads=[f"gs{nh}", f"hb{nh}"], writes=[f"hb{nh}"])
                pop4()
                T.emit("vector", lambda v, h0=h0: v.tensor_max(
                    M43[:, h0:h0 + 1024], M43[:, h0:h0 + 1024], hb[:, h0:h0 + 1024]),
                    reads=[f"hb{nh}", f"m43h{nh}"], writes=[f"m43h{nh}"])
                pop4()
                T.emit("vector", lambda v, h0=h0: v.tensor_tensor(
                    M43N[:, h0:h0 + 1024], M43N[:, h0:h0 + 1024],
                    hb[:, h0:h0 + 1024], op=ALU.min),
                    reads=[f"hb{nh}", f"m43n{nh}"], writes=[f"m43n{nh}"])
                pop4()
            while pend4:
                pend4.pop(0)()
            if fc + 1 < NFC:
                gateup_finish(0, wqT32g, "wqT32g", wqbg, "wqbg", (0, 1))
                gateup_finish(1, wqT32u, "wqT32u", wqbu, "wqbu", (2, 3))
            T.dma("sync", hs[fc, :, :], hb[:], reads=["hb0", "hb1"],
                  writes=[f"hsc{fc}"], tag=f"sth{fc}")

        dbg_dump(T, ["hs"])

        T.emit("vector", lambda v: v.tensor_scalar(
            M43N[:], M43N[:], -1.0, None, op0=ALU.mult),
            reads=["m43n0", "m43n1"], writes=["m43n0", "m43n1"])
        T.emit("vector", lambda v: v.tensor_max(M43[:], M43[:], M43N[:]),
               reads=["m43n0", "m43n1", "m43h0", "m43h1"],
               writes=["m43h0", "m43h1"])
        for ts in range(16):
            T.emit("tensor", lambda t, ts=ts: t.transpose(
                psb(0)[:, 0:128], M43[:, 128 * ts:128 * (ts + 1)], ident),
                reads=["m43h0", "m43h1", "co"], writes=["psB0"])
            T.emit("vector", lambda v, ts=ts: v.reduce_max(
                SM2[:, ts:ts + 1], psb(0)[:, 0:128], axis=AX.X),
                reads=["psB0"], writes=["amax2", "q_s0", "q_sc"])
        T.emit("vector", lambda v: v.tensor_scalar(
            SM2[:, 0:16], SM2[:, 0:16], EPS, None, op0=ALU.max),
            reads=["amax2"], writes=["amax2"])
        T.emit("tensor", lambda t: t.transpose(psb(1)[:, 0:128], SM2[:, 0:128], ident),
               reads=["amax2", "co"], writes=["psB1"])
        T.emit("vector", lambda v: v.tensor_copy(SM2[:, 128:256], psb(1)[:, 0:128]),
               reads=["psB1"], writes=["amax2T", "q_thr", "q_nthr"])
        T.phase()

        sstg4 = [AR1[:, 2048 * i:2048 * (i + 1)] for i in range(2)]
        ltmp = AR1[:, 4096:8192]
        outtm = AR1[:, 8192:16384]
        outtm3 = outtm.rearrange("p (c n) -> p c n", c=4)
        s2rep = AR1[:, 16384:18432]
        is2rep = AR1[:, 18432:20480]
        hld = [AR1[:, 20480 + 512 * i:20480 + 512 * (i + 1)] for i in range(2)]
        x1l = [AR1[:, 21504 + 512 * i:21504 + 512 * (i + 1)] for i in range(2)]
        otmp2 = [AR1[:, 22528 + 512 * i:22528 + 512 * (i + 1)] for i in range(2)]
        hqb3 = BFB[:, 0:22016].rearrange("p (c n) -> p c n", c=NFC)
        wq4 = [[BFB[:, 22016 + 8192 * s2_ + 1024 * l_:
                    22016 + 8192 * s2_ + 1024 * (l_ + 1)]
                for l_ in range(8)] for s2_ in range(2)]

        def lanebank(l):
            return PSA[:, 512 * l:512 * (l + 1)] if l < 4 else \
                PSB[:, 512 * (l - 4):512 * (l - 3)]

        def lkey(l):
            return f"psA{l}" if l < 4 else f"psB{l - 4}"

        with T.dma_batch("gpsimd"):
            for c in range(16):
                T.dma("gpsimd", s2rep[0:1, 128 * c:128 * (c + 1)], SM2[c:c + 1, 128:256],
                      reads=["amax2T"], writes=["amax2F"], tag=f"af2_{c}")
        for ns in range(4):
            T.emit("tensor", lambda t, ns=ns: t.matmul(
                PSA[:, 512 * ns:512 * (ns + 1)], ONES[0:1, :],
                s2rep[0:1, 512 * ns:512 * (ns + 1)], start=True, stop=True),
                reads=["amax2F", "ones"], writes=[f"psA{ns}"], tag=f"rep2_{ns}")
        T.emit("vector", lambda v: v.tensor_scalar(
            is2rep[:], PSA[:], 1.0 / 127.0, None, op0=ALU.mult),
            reads=["psA0", "psA1", "psA2", "psA3"], writes=["is2"])
        T.emit("vector", lambda v: v.reciprocal(s2rep[:], is2rep[:]),
               reads=["is2"], writes=["s2"])

        NCH = (NFC + 3) // 4

        def chdim(ch):
            fc0 = 4 * ch
            return fc0, min(4, NFC - fc0)

        def emit_ladder(g, ch):
            fc0, fcn = chdim(ch)
            par = ch % 2
            sstg = sstg4[par]
            for i in range(fcn):
                fc = fc0 + i
                hb2 = i % 2
                T.dma("gpsimd", hld[hb2][:], hs[fc, :, 512 * g:512 * (g + 1)],
                      reads=[f"hsc{fc}"], writes=[f"hld{hb2}"], tag=f"ldh{g}_{fc}")
                T.emit("vector", lambda v, i=i, hb2=hb2, g=g, sstg=sstg: v.tensor_mul(
                    sstg[:, 512 * i:512 * (i + 1)], hld[hb2][:],
                    s2rep[:, 512 * g:512 * (g + 1)]),
                    reads=[f"hld{hb2}", "s2"], writes=[f"sstg{par}"])
            width = 512 * fcn
            ladder(sstg[:, 0:width], ltmp[:, 0:2 * width], f"sstg{par}", width,
                   tmp_key="ltmp", out=BFB[:, 512 * fc0:512 * fc0 + width],
                   out_key=f"hqc{ch}")

        def emit_half(g, ch, dc0):
            fc0, fcn = chdim(ch)
            par = ch % 2
            with T.dma_batch("sync"):
                for l in range(8):
                    dc = dc0 + l
                    for hl in range(2):
                        T.dma("sync", wq4[par][l][:, fcn * 128 * hl:fcn * 128 * (hl + 1)]
                              .rearrange("p (c n) -> p c n", c=fcn),
                              wds[dc, hl, fc0:fc0 + fcn, :, :]
                              .rearrange("c p n -> p c n"),
                              reads=[f"wdsc{dc}"], writes=[f"wq{par}_{l}"],
                              tag=f"ldw_{g}_{ch}_{dc}_{hl}")
            for l in range(8):
                for i in range(fcn):
                    for hl in range(2):
                        off = 128 * (fcn * hl + i)
                        T.emit("tensor", lambda t, l=l, par=par, off=off, i=i,
                               hl=hl, fc0=fc0, ch=ch, fcn=fcn: t.matmul(
                            lanebank(l), wq4[par][l][:, off:off + 128],
                            hqb3[:, fc0 + i, :],
                            start=(ch == 0 and i == 0 and hl == 0),
                            stop=(ch == NCH - 1 and i == fcn - 1 and hl == 1)),
                            reads=[f"wq{par}_{l}", f"hqc{ch}"], writes=[lkey(l)],
                            tag=f"dm_{g}_{dc0 + l}")

        def evac_lane(g, dc, l):
            wb = l % 2
            T.dma("gpsimd", x1l[wb][:], x1s[g, dc, :, :], reads=[f"x1sc{g}"],
                  writes=[f"x1l{wb}"], tag=f"ldx1{g}_{dc}")
            T.emit("vector", lambda v, l=l, g=g, wb=wb: v.tensor_mul(
                otmp2[wb][:], lanebank(l)[:, 0:512],
                is2rep[:, 512 * g:512 * (g + 1)]),
                reads=[lkey(l), "is2"], writes=[f"otmp{wb}"])
            T.emit("vector", lambda v, wb=wb: v.tensor_add(
                otmp2[wb][:], otmp2[wb][:], x1l[wb][:]),
                reads=[f"otmp{wb}", f"x1l{wb}"], writes=[f"otmp{wb}"])
            for ff in range(4):
                T.emit("tensor", lambda t, l=l, wb=wb, ff=ff: t.transpose(
                    lanebank(l)[:, 128 * ff:128 * (ff + 1)],
                    otmp2[wb][:, 128 * ff:128 * (ff + 1)], ident),
                    reads=[f"otmp{wb}", "co"], writes=[lkey(l)])
            T.emit("scalar", lambda s, l=l, dc=dc: s.activation(
                outtm3[:, 0:4, 128 * dc:128 * (dc + 1)],
                lanebank(l)[:, 0:512].rearrange("p (c n) -> p c n", c=4), AF.Copy),
                reads=[lkey(l)], writes=["outtm"])

        def store_out(g):
            tok0 = 512 * g
            T.dma("sync", out_h[tok0:tok0 + 512, :].rearrange("(c p) n -> p c n", p=128),
                  outtm3[:, :, :], reads=["outtm"], writes=["out_h"], tag=f"stout{g}")

        for ch in range(5):
            emit_ladder(0, ch)
        for g in range(4):
            if g > 0:
                for l in range(8):
                    evac_lane(g - 1, 8 + l, l)
                store_out(g - 1)
            for ch in range(NCH):
                if ch >= 5:
                    emit_ladder(g, ch)
                emit_half(g, ch, 0)
            for l in range(8):
                evac_lane(g, l, l)
            for ch in range(NCH):
                emit_half(g, ch, 8)
                if g < 3 and ch <= 4:
                    emit_ladder(g + 1, ch)
        for l in range(8):
            evac_lane(3, 8 + l, l)
        store_out(3)

        T.barrier("sync")
        T.build()
        BFB_ctx.__exit__(None, None, None)
    return nc


def kernel(L0_attn_hidden_states, L0_attn_Wq, L0_attn_Wk, L0_attn_Wv, L0_attn_Wo,
           L0_ffn_w_gate, L0_ffn_w_up, L0_ffn_w_down, _dbg=(), _cores=8):
    x = np.ascontiguousarray(np.asarray(L0_attn_hidden_states, dtype=np.float32))
    consts = host_consts()
    wq = np.ascontiguousarray(np.asarray(L0_attn_Wq, np.float32))
    wk = np.ascontiguousarray(np.asarray(L0_attn_Wk, np.float32))
    wv = np.ascontiguousarray(np.asarray(L0_attn_Wv, np.float32))
    wo = np.ascontiguousarray(np.asarray(L0_attn_Wo, np.float32))
    wg = np.ascontiguousarray(np.asarray(L0_ffn_w_gate, np.float32))
    wu = np.ascontiguousarray(np.asarray(L0_ffn_w_up, np.float32))
    wd = np.ascontiguousarray(np.asarray(L0_ffn_w_down, np.float32))
    nc = build_program(dbg=_dbg)
    in_maps = [{"x": np.ascontiguousarray(x[b]), "wq": wq, "wk": wk, "wv": wv,
                "wo": wo, "wg": wg, "wu": wu, "wd": wd, "consts": consts}
               for b in range(_cores)]
    res = run_bass_kernel_spmd(nc, in_maps, core_ids=list(range(_cores)))
    out = np.stack([res.results[b]["out"] for b in range(_cores)], axis=0)
    if _dbg:
        return out, res
    return out
